# version 33
# baseline (speedup 1.0000x reference)
import numpy as np
from contextlib import ExitStack

import concourse.bass as bass
import concourse.mybir as mybir
from concourse.bass_utils import run_bass_kernel_spmd

F32 = mybir.dt.float32
BF16 = mybir.dt.bfloat16
AF = mybir.ActivationFunctionType
ALU = mybir.AluOpType
AX = mybir.AxisListType

D = 2048
DL = 1024
T = 4096
NST = 16
RK = 64
TQ = 1024
NQ = T // TQ
NDC = DL // 128
NKC = D // 128
NTT = T // 128
EPS = 1e-5
NCORES = 8


class Buf:
    __slots__ = ("writer", "readers")

    def __init__(self):
        self.writer = None
        self.readers = []


class Sched:
    ENGS = ("sync", "scalar", "vector", "tensor", "gpsimd")
    SEMS = [e for e in ENGS] + [e + "&d" for e in ENGS]

    def __init__(self, nc):
        self.nc = nc
        self.items = {e: [] for e in self.ENGS}
        self.count = {s: 0 for s in self.SEMS}
        self.sems = {}

    def make_sems(self, es):
        for s in self.SEMS:
            self.sems[s] = es.enter_context(
                self.nc.semaphore("sem_" + s.replace("&", "_")))

    def op(self, eng, fn, reads=(), writes=(), dma=False, extra=()):
        waits = list(extra)
        for b in reads:
            if b.writer is not None:
                waits.append(b.writer)
        for b in writes:
            if b.writer is not None:
                waits.append(b.writer)
            waits.extend(b.readers)
        sem = eng + "&d" if dma else eng
        inc = 16 if dma else 1
        tok = (sem, self.count[sem] + inc)
        best = {}
        for (ws, wv) in waits:
            if eng == "tensor" and ws == "tensor":
                continue
            if ws.endswith("&d"):
                wv = self.count[ws]
            if wv > best.get(ws, -1):
                best[ws] = wv
        self.items[eng].append((tuple(best.items()), fn, inc, sem))
        self.count[sem] += inc
        for b in reads:
            b.readers.append(tok)
        for b in writes:
            b.writer = tok
            b.readers = []
        return tok

    def barrier(self):
        snap = [(s, self.count[s]) for s in self.SEMS if self.count[s]]
        for e in self.ENGS:
            self.op(e, lambda eng: eng.nop(), [], [], extra=snap)

    def build(self, block):
        for eng in self.ENGS:
            items = self.items[eng]
            if not items:
                continue

            def make(items):
                def section(engine):
                    waited = {s: 0 for s in self.SEMS}
                    for waits, fn, inc, sem in items:
                        for (ws, wv) in waits:
                            if wv > waited[ws]:
                                engine.wait_ge(self.sems[ws], wv)
                                waited[ws] = wv
                        fn(engine).then_inc(self.sems[sem], inc)
                return section

            getattr(block, eng)(make(items))


def build_nc(taps=()):
    nc = bass.Bass("TRN2", target_bir_lowering=False, debug=False)
    dt = nc.dram_tensor

    xT = dt("xT", [D, T], F32, kind="ExternalInput").ap()
    xN = dt("xN", [T, DL], F32, kind="ExternalInput").ap()
    wg1 = dt("wg1", [128, NKC * 128], F32, kind="ExternalInput").ap()
    wg2 = dt("wg2", [128, NKC * 128], F32, kind="ExternalInput").ap()
    dtw = dt("dtw", [RK, DL], F32, kind="ExternalInput").ap()
    dgt = dt("dgt", [64, DL], F32, kind="ExternalInput").ap()
    bwl = dt("bwl", [8, DL], F32, kind="ExternalInput").ap()
    mdTd = dt("mdTd", [128, NKC * 8], F32, kind="ExternalInput").ap()
    dbTd = dt("dbTd", [128, NKC * 8], F32, kind="ExternalInput").ap()
    alog = dt("alog", [128, NDC * NST], F32, kind="ExternalInput").ap()
    dtb = dt("dtb", [128, NDC], F32, kind="ExternalInput").ap()
    dvec = dt("dvec", [128, NDC], F32, kind="ExternalInput").ap()
    slp = dt("slp", [8, 1], F32, kind="ExternalInput").ap()
    ctr = dt("ctr", [8, 1], F32, kind="ExternalInput").ap()
    iden = dt("iden", [128, 128], F32, kind="ExternalInput").ap()
    iota16 = dt("iota16", [128, 8], F32, kind="ExternalInput").ap()
    onesc = dt("onesc", [128, 1], F32, kind="ExternalInput").ap()
    sel2 = dt("sel2", [2, 256], F32, kind="ExternalInput").ap()
    rep8 = dt("rep8", [8, 64], F32, kind="ExternalInput").ap()
    sel16 = dt("sel16", [16, 16 * 128], F32, kind="ExternalInput").ap()
    sels = dt("sels", [64, 8], F32, kind="ExternalInput").ap()

    out = dt("out", [T, DL], F32, kind="ExternalOutput").ap()
    tap_aps = {}
    for tspec in taps:
        name, shape = tspec[0], tspec[1]
        tdt = BF16 if (len(tspec) > 2 and tspec[2] == "bf16") else F32
        tap_aps[name] = dt(name, list(shape), tdt, kind="ExternalOutput").ap()

    es = ExitStack()
    BUFS = {}
    sch = Sched(nc)

    def _alloc(stack, kind, name, shape, dtype=F32):
        if kind == "sbuf":
            h = stack.enter_context(nc.sbuf_tensor(name, shape, dtype))
        else:
            h = stack.enter_context(nc.psum_tensor(name, shape, dtype))
        BUFS[name] = Buf()
        return h

    def S(name, shape, dtype=F32):
        return _alloc(es, "sbuf", name, shape, dtype)

    def PSUM(name, shape):
        return _alloc(es, "psum", name, shape, F32)

    def bl(hs):
        return [BUFS[h.name] for h in hs]

    sel16b = S("sel16b", [16, 16 * 128], BF16)
    dtwb = S("dtwb", [RK, DL], BF16)
    dgtb = S("dgtb", [64, DL], BF16)
    bwlb = S("bwlb", [8, DL], BF16)
    anegs = S("anegs", [128, NDC * NST])
    negb = S("negb", [128, NDC])
    posb = S("posb", [128, NDC])
    dvecf = S("dvecf", [128, NDC])
    slpf = S("slpf", [8, 1]); ctrf = S("ctrf", [8, 1]); nslpc = S("nslpc", [8, 1])
    idenf = S("idenf", [128, 128]); idenb = S("idenb", [128, 128], BF16)
    iotaf = S("iotaf", [128, 8])
    onescf = S("onescf", [128, 1]); onescb = S("onescb", [128, 1], BF16)
    nhalfc = S("nhalfc", [128, 1])
    sel2f = S("sel2f", [2, 256])
    rep8b = S("rep8b", [8, 64], BF16)
    selsf = S("selsf", [64, 8])
    pack2 = S("pack2", [2, T])
    Sbf1 = S("Sbf1", [128, T], BF16)
    mtil = Sbf1[0:64, :]
    bnegb = Sbf1[64:80, :]
    Sbf2 = S("Sbf2", [128, T], BF16)
    xdbr = Sbf2[0:64, :]
    Sbf3 = S("Sbf3", [128, T], BF16)
    oh1 = Sbf3[0:8, :]
    crowb = Sbf3[32:48, :]
    statc = S("statc", [128, 2 * NTT])
    stcol = [Buf() for _ in range(NDC * NST)]
    ssc = S("ssc", [128, NTT])
    idxc = S("idxc", [128, NTT])
    st_ = S("st_", [128, NDC * NST])

    pY0 = PSUM("pY0", [128, 1024])
    pY1 = PSUM("pY1", [128, 1024])
    pDP = PSUM("pDP", [128, 1024])
    pTR = PSUM("pTR", [128, 512])
    pSM = PSUM("pSM", [128, 512])

    with nc.Block() as block:
        sch.make_sems(es)

        def DMA(q, out_ap, in_ap, r=(), w=()):
            return sch.op(q, lambda e: e.dma_start(out=out_ap, in_=in_ap),
                          bl(r), bl(w), dma=True)

        def ACT(out_ap, in_ap, func, r=(), w=(), bias=None, scale=None):
            kw = {}
            if bias is not None:
                kw["bias"] = bias
            if scale is not None:
                kw["scale"] = scale
            return sch.op("scalar", lambda e: e.activation(
                out=out_ap, in_=in_ap, func=func, **kw), bl(r), bl(w))

        def VTT(out_ap, a, b_, alu, r=(), w=()):
            return sch.op("vector", lambda e: e.tensor_tensor(
                out=out_ap, in0=a, in1=b_, op=alu), bl(r), bl(w))

        def VTS(out_ap, in_ap, s1, op0, r=(), w=(), s2=None, op1=None):
            if op1 is None:
                return sch.op("vector", lambda e: e.tensor_scalar(
                    out=out_ap, in0=in_ap, scalar1=s1, scalar2=None, op0=op0),
                    bl(r), bl(w))
            return sch.op("vector", lambda e: e.tensor_scalar(
                out=out_ap, in0=in_ap, scalar1=s1, scalar2=s2, op0=op0, op1=op1),
                bl(r), bl(w))

        def VSTT(out_ap, in0, s, in1, op0, op1, r=(), w=()):
            return sch.op("vector", lambda e: e.scalar_tensor_tensor(
                out=out_ap, in0=in0, scalar=s, in1=in1, op0=op0, op1=op1),
                bl(r), bl(w))

        def VRED(out_ap, in_ap, alu, r=(), w=()):
            return sch.op("vector", lambda e: e.tensor_reduce(
                out=out_ap, in_=in_ap, axis=AX.X, op=alu), bl(r), bl(w))

        def VCOPY(out_ap, in_ap, r=(), w=()):
            return sch.op("vector", lambda e: e.tensor_copy(
                out=out_ap, in_=in_ap), bl(r), bl(w))

        def VREC(out_ap, in_ap, r=(), w=()):
            return sch.op("vector", lambda e: e.reciprocal(
                out=out_ap, in_=in_ap), bl(r), bl(w))

        def VSCAN(out_ap, d0, d1, init, r=(), w=()):
            return sch.op("vector", lambda e: e.tensor_tensor_scan(
                out=out_ap, data0=d0, data1=d1, initial=init,
                op0=ALU.mult, op1=ALU.add), bl(r), bl(w))

        def MM(out_ap, lhsT, rhs, start, stop, r=(), w=()):
            return sch.op("tensor", lambda e: e.matmul(
                out=out_ap, lhsT=lhsT, rhs=rhs, start=start, stop=stop),
                bl(r), bl(w))

        def TRANS(out_ap, in_ap, r=(), w=()):
            p = in_ap.shape[0]
            return sch.op("tensor", lambda e: e.transpose(
                out=out_ap, in_=in_ap, identity=idenf[0:p, 0:p]),
                bl(r) + [BUFS["idenf"]], bl(w))

        def MEMSET(out_ap, val, w=()):
            return sch.op("vector", lambda e: e.memset(out_ap, val), [], bl(w))

        def ln_chain(t0, t1, tmp, mu_out=None):
            VTS(t1[:], t1[:], 1.0 / D, ALU.mult, r=[t1], w=[t1])
            if mu_out is not None:
                VCOPY(mu_out[:], t1[:], r=[t1], w=[mu_out])
            VTT(tmp[:], t1[:], t1[:], ALU.mult, r=[t1], w=[tmp])
            VSTT(t0[:], t0[:], 1.0 / D, tmp[:], ALU.mult, ALU.subtract,
                 r=[t0, tmp], w=[t0])
            VTS(t0[:], t0[:], EPS, ALU.add, r=[t0], w=[t0])
            ACT(t0[:], t0[:], AF.Ln, r=[t0], w=[t0])
            ACT(t0[:], t0[:], AF.Exp, scale=nhalfc[:], r=[t0, nhalfc], w=[t0])
            VTT(t1[:], t1[:], t0[:], ALU.mult, r=[t1, t0], w=[t1])

        def bcast2(dst, qoff):
            for j, d_ in reversed(list(enumerate(dst))):
                for hh in range(2):
                    ssl = slice(qoff + hh * 512, qoff + hh * 512 + 512)
                    pb = pTR if hh == 0 else pSM
                    MM(pb[:, 0:512], sel2f[:, j * 128:(j + 1) * 128],
                       pack2[0:2, ssl], True, True, r=[sel2f, pack2], w=[pb])
                    ACT(d_[:, hh * 512:hh * 512 + 512], pb[:, 0:512], AF.Copy,
                        r=[pb], w=[d_])

        esA = ExitStack()
        SA = lambda n, sh, dtp=F32: _alloc(esA, "sbuf", n, sh, dtp)
        sel16f = SA("sel16f", [16, 16 * 128])
        dtwf = SA("dtwf", [RK, DL])
        dgtf = SA("dgtf", [64, DL])
        bwlf = SA("bwlf", [8, DL])
        alogf = SA("alogf", [128, NDC * NST])
        dtbf = SA("dtbf", [128, NDC])
        rep8f = SA("rep8f", [8, 64])

        DMA("sync", sel16f[:], sel16, w=[sel16f])
        ACT(sel16b[:], sel16f[:], AF.Copy, r=[sel16f], w=[sel16b])
        DMA("sync", dtwf[:], dtw, w=[dtwf])
        DMA("sync", dgtf[:], dgt, w=[dgtf])
        DMA("sync", bwlf[:], bwl, w=[bwlf])
        DMA("sync", alogf[:], alog, w=[alogf])
        DMA("sync", dtbf[:], dtb, w=[dtbf])
        DMA("sync", dvecf[:], dvec, w=[dvecf])
        DMA("sync", slpf[:], slp, w=[slpf])
        DMA("sync", ctrf[:], ctr, w=[ctrf])
        DMA("sync", idenf[:], iden, w=[idenf])
        DMA("sync", iotaf[:], iota16, w=[iotaf])
        DMA("sync", onescf[:], onesc, w=[onescf])
        DMA("sync", sel2f[:], sel2, w=[sel2f])
        DMA("sync", rep8f[:], rep8, w=[rep8f])
        DMA("sync", selsf[:], sels, w=[selsf])

        ACT(dtwb[:], dtwf[:], AF.Copy, r=[dtwf], w=[dtwb])
        ACT(dgtb[:], dgtf[:], AF.Copy, r=[dgtf], w=[dgtb])
        ACT(bwlb[:], bwlf[:], AF.Copy, r=[bwlf], w=[bwlb])
        ACT(anegs[:], alogf[:], AF.Exp, r=[alogf], w=[anegs])
        VTS(anegs[:], anegs[:], -1.0, ALU.mult, r=[anegs], w=[anegs])
        ACT(posb[:], dtbf[:], AF.Copy, r=[dtbf], w=[posb])
        ACT(onescb[:], onescf[:], AF.Copy, r=[onescf], w=[onescb])
        ACT(idenb[:], idenf[:], AF.Copy, r=[idenf], w=[idenb])
        ACT(rep8b[:], rep8f[:], AF.Copy, r=[rep8f], w=[rep8b])
        VTS(negb[:], dtbf[:], -1.0, ALU.mult, r=[dtbf], w=[negb])
        VTS(nhalfc[:], onescf[:], -0.5, ALU.mult, r=[onescf], w=[nhalfc])
        VTT(nslpc[:], slpf[:], ctrf[:], ALU.mult, r=[slpf, ctrf], w=[nslpc])
        VTS(nslpc[:], nslpc[:], -1.0, ALU.mult, r=[nslpc], w=[nslpc])
        sch.op("vector", lambda e: e.memset(st_[:], 0.0), [],
               [BUFS["st_"]] + stcol)

        sch.barrier()
        esA.close()

        esB = ExitStack()
        SB = lambda n, sh, dtp=F32: _alloc(esB, "sbuf", n, sh, dtp)
        wg1f = SB("wg1f", [128, NKC * 128])
        wg2f = SB("wg2f", [128, NKC * 128])
        wg1b = SB("wg1b", [128, NKC * 128], BF16)
        wg2b = SB("wg2b", [128, NKC * 128], BF16)
        DMA("sync", wg1f[:], wg1, w=[wg1f])
        DMA("sync", wg2f[:], wg2, w=[wg2f])
        ACT(wg1b[:], wg1f[:], AF.Copy, r=[wg1f], w=[wg1b])
        ACT(wg2b[:], wg2f[:], AF.Copy, r=[wg2f], w=[wg2b])
        mdTf = SB("mdTf", [128, NKC * 8])
        dbTf = SB("dbTf", [128, NKC * 8])
        mdTb = SB("mdTb", [128, NKC * 8], BF16)
        dbTb = SB("dbTb", [128, NKC * 8], BF16)
        mdsqb = SB("mdsqb", [128, NKC * 8], BF16)
        DMA("sync", mdTf[:], mdTd, w=[mdTf])
        DMA("sync", dbTf[:], dbTd, w=[dbTf])
        ACT(mdTb[:], mdTf[:], AF.Copy, r=[mdTf], w=[mdTb])
        ACT(dbTb[:], dbTf[:], AF.Copy, r=[dbTf], w=[dbTb])
        ACT(mdsqb[:], mdTf[:], AF.Square, r=[mdTf], w=[mdsqb])
        AB = SB("AB", [128, T])
        CD = SB("CD", [128, T])
        S1 = SB("S1", [128, T])
        S2 = SB("S2", [128, T])
        rp = [SB(f"rp{i}", [128, TQ]) for i in range(4)]
        xin = [SB(f"xin{i}", [128, TQ]) for i in range(3)]
        xbf = [SB(f"xbf{i}", [128, TQ], BF16) for i in range(3)]
        xsq = [SB(f"xsq{i}", [128, TQ], BF16) for i in range(3)]
        gT = SB("gT", [128, 8 * NTT])
        ohT = SB("ohT", [128, 8 * NTT])
        g_v = SB("g_v", [128, 8 * NTT])
        cross = SB("cross", [64, 16])
        c1c2 = SB("c1c2", [64, 2])
        ws_g1 = SB("ws_g1", [128, 1]); ws_g2 = SB("ws_g2", [128, 1])
        nws_g1 = SB("nws_g1", [128, 1]); nws_g2 = SB("nws_g2", [128, 1])
        nws2 = SB("nws2", [128, 1])
        nws3 = SB("nws3", [128, 1])
        mdsqs = SB("mdsqs", [8, 1])

        a_lo = AB[0:64, :]
        a_hi = AB[64:128, :]
        a_ws = CD[0:64, :]
        a_u6 = CD[64:128, :]
        arr_C = S1[32:48, :]
        g_nsq = S1[0:8, :]
        arr_x8 = S2[32:40, :]
        arr_x8c = S2[0:8, :]
        g_gate = S2[0:8, :]
        g_sc = AB[0:8, :]

        for col, wsrc in ((0, wg1b), (1, wg2b)):
            for k in range(NKC):
                MM(pSM[:, col:col + 1], wsrc[:, k * 128:(k + 1) * 128],
                   onescb[:, :], k == 0, k == NKC - 1, r=[wsrc, onescb], w=[pSM])
        for col, wsrc in ((2, mdTb), (3, mdsqb)):
            for k in range(NKC):
                MM(pSM[0:8, col:col + 1], wsrc[:, k * 8:(k + 1) * 8],
                   onescb[:, :], k == 0, k == NKC - 1, r=[wsrc, onescb], w=[pSM])
        ACT(ws_g1[:, 0:1], pSM[:, 0:1], AF.Copy, r=[pSM], w=[ws_g1])
        ACT(ws_g2[:, 0:1], pSM[:, 1:2], AF.Copy, r=[pSM], w=[ws_g2])
        ACT(mdsqs[:, 0:1], pSM[0:8, 3:4], AF.Copy, r=[pSM], w=[mdsqs])
        VTS(nws_g1[:], ws_g1[:], -1.0, ALU.mult, r=[ws_g1], w=[nws_g1])
        VTS(nws_g2[:], ws_g2[:], -1.0, ALU.mult, r=[ws_g2], w=[nws_g2])
        DMA("sync", nws2[32:48, :], nws_g1[80:96, :], r=[nws_g1], w=[nws2])
        DMA("sync", nws3[32:40, :], nws_g1[96:104, :], r=[nws_g1], w=[nws3])

        for k in range(NKC):
            MM(pTR[0:64, 0:8], wg2b[:, k * 128:k * 128 + 64],
               mdTb[:, k * 8:(k + 1) * 8], k == 0, k == NKC - 1,
               r=[wg2b, mdTb], w=[pTR])
        for k in range(NKC):
            MM(pTR[0:64, 8:16], wg2b[:, k * 128 + 64:(k + 1) * 128],
               dbTb[:, k * 8:(k + 1) * 8], k == 0, k == NKC - 1,
               r=[wg2b, dbTb], w=[pTR])
        ACT(cross[:, :], pTR[0:64, 0:16], AF.Copy, r=[pTR], w=[cross])
        for n in range(8):
            DMA("sync", c1c2[n * 8:(n + 1) * 8, 0:1],
                cross[n * 8:(n + 1) * 8, n:n + 1], r=[cross], w=[c1c2])
            DMA("sync", c1c2[n * 8:(n + 1) * 8, 1:2],
                cross[n * 8:(n + 1) * 8, 8 + n:9 + n], r=[cross], w=[c1c2])

        for tcg in range(4):
            for k in range(NKC):
                gi = tcg * NKC + k
                par = gi % 3
                DMA(("sync", "scalar", "gpsimd")[par], xin[par][:],
                    xT[k * 128:(k + 1) * 128, tcg * 1024:(tcg + 1) * 1024],
                    w=[xin[par]])
                ACT(xbf[par][:], xin[par][:], AF.Copy,
                    r=[xin[par]], w=[xbf[par]])
                VTT(xsq[par][:], xin[par][:], xin[par][:], ALU.mult,
                    r=[xin[par]], w=[xsq[par]])
                st0, sp0 = (k == 0), (k == NKC - 1)
                for hh in range(2):
                    hsl = slice(hh * 512, hh * 512 + 512)
                    MM(pY0[:, hsl], wg1b[:, k * 128:(k + 1) * 128],
                       xbf[par][:, hsl], st0, sp0, r=[wg1b, xbf[par]], w=[pY0])
                    MM(pY1[:, hsl], wg2b[:, k * 128:(k + 1) * 128],
                       xbf[par][:, hsl], st0, sp0, r=[wg2b, xbf[par]], w=[pY1])
                    MM(pDP[0:1, hsl], onescb[:, :], xsq[par][:, hsl],
                       st0, sp0, r=[onescb, xsq[par]], w=[pDP])
            for hh in range(2):
                dsl = slice(tcg * 1024 + hh * 512, tcg * 1024 + hh * 512 + 512)
                hsl = slice(hh * 512, hh * 512 + 512)
                ACT(AB[0:64, dsl], pY0[0:64, hsl], AF.Copy, r=[pY0], w=[AB])
                ACT(AB[64:128, dsl], pY0[64:128, hsl], AF.Copy, r=[pY0], w=[AB])
                ACT(CD[0:64, dsl], pY1[0:64, hsl], AF.Copy, r=[pY1], w=[CD])
                ACT(CD[64:128, dsl], pY1[64:128, hsl], AF.Copy, r=[pY1], w=[CD])
                ACT(pack2[0:1, dsl], pDP[0:1, hsl], AF.Copy, r=[pDP], w=[pack2])

        DMA("sync", arr_C, a_hi[16:32, :], r=[AB], w=[S1])
        DMA("sync", arr_x8, a_hi[32:40, :], r=[AB], w=[S2])
        DMA("sync", pack2[1:2, :], a_hi[40:41, :], r=[AB], w=[pack2])

        for q in range(NQ):
            qoff = q * TQ
            tsl = slice(qoff, qoff + TQ)
            bcast2([rp[0], rp[1]], qoff)
            ln_chain(rp[0], rp[1], rp[2], mu_out=rp[3])
            for (arr_ap, stripe, base, rows, nws_ap, nws_h) in (
                    (a_lo, AB, 0, 64, nws_g1[0:64, 0:1], nws_g1),
                    (a_hi[0:16, :], AB, 64, 16, nws_g1[64:80, 0:1], nws_g1),
                    (arr_C, S1, 32, 16, nws2[32:48, 0:1], nws2),
                    (arr_x8, S2, 32, 8, nws3[32:40, 0:1], nws3),
                    (a_ws, CD, 0, 64, nws_g2[0:64, 0:1], nws_g2),
                    (a_u6, CD, 64, 64, nws_g2[64:128, 0:1], nws_g2)):
                VSTT(arr_ap[:, tsl], rp[3][base:base + rows, :], nws_ap,
                     arr_ap[:, tsl], ALU.mult, ALU.add,
                     r=[rp[3], stripe, nws_h], w=[stripe])
                VTT(arr_ap[:, tsl], arr_ap[:, tsl], rp[0][base:base + rows, :],
                    ALU.mult, r=[stripe, rp[0]], w=[stripe])

        VTS(bnegb, a_hi[0:16, :], 1.0, ALU.mult, r=[AB], w=[Sbf1])
        ACT(crowb, arr_C, AF.Copy, r=[S1], w=[Sbf3])
        ACT(xdbr, a_lo, AF.Copy, r=[AB], w=[Sbf2])

        VTS(a_ws, a_ws, c1c2[:, 0:1], ALU.subtract, r=[CD, c1c2], w=[CD])
        ACT(a_ws, a_ws, AF.Square, r=[CD], w=[CD])
        for c in range(8):
            csl = slice(c * 512, (c + 1) * 512)
            MM(pSM[0:8, 0:512], selsf[:, :], a_ws[:, csl], True, True,
               r=[selsf, CD], w=[pSM])
            ACT(g_sc[0:8, csl], pSM[0:8, 0:512], AF.Copy, r=[pSM], w=[AB])
        DMA("sync", arr_x8c, arr_x8, r=[S2], w=[S2])
        for q in range(NQ):
            qoff = q * TQ
            bcast2([rp[0], rp[1]], qoff)
            ln_chain(rp[0], rp[1], rp[2], mu_out=rp[3])
            bcast2([rp[2]], qoff)
            VTT(rp[3][:], rp[3][:], rp[3][:], ALU.mult, r=[rp[3]], w=[rp[3]])
            VSTT(rp[2][:], rp[3][:], -float(D), rp[2][:], ALU.mult, ALU.add,
                 r=[rp[3], rp[2]], w=[rp[2]])
            VTT(rp[2][:], rp[2][:], rp[0][:], ALU.mult, r=[rp[2], rp[0]], w=[rp[2]])
            VTT(rp[2][:], rp[2][:], rp[0][:], ALU.mult, r=[rp[2], rp[0]], w=[rp[2]])
            for hh in range(2):
                csl = slice(qoff + hh * 512, qoff + hh * 512 + 512)
                rsl = slice(hh * 512, hh * 512 + 512)
                VSTT(g_nsq[:, csl], arr_x8c[:, csl], -2.0, rp[2][0:8, rsl],
                     ALU.mult, ALU.add, r=[S2, rp[2]], w=[S1])
        VTS(g_nsq, g_nsq, mdsqs[:, 0:1], ALU.add, r=[S1, mdsqs], w=[S1])
        VREC(g_nsq, g_nsq, r=[S1], w=[S1])
        VTT(g_sc, g_sc, g_nsq, ALU.mult, r=[AB, S1], w=[AB])
        ACT(g_gate, g_sc, AF.Sigmoid, r=[AB, slpf, nslpc], w=[S2],
            scale=slpf[:, 0:1], bias=nslpc[:, 0:1])

        for half in range(2):
            for j in range(16):
                c = half * 16 + j
                TRANS(pTR[0:128, j * 8:j * 8 + 8], g_gate[:, c * 128:(c + 1) * 128],
                      r=[S2], w=[pTR])
            ACT(gT[:, half * 128:half * 128 + 128], pTR[:, 0:128], AF.Copy,
                r=[pTR], w=[gT])
        for c in range(NTT):
            tg = gT[:, c * 8:(c + 1) * 8]
            VRED(ssc[:, c:c + 1], tg, ALU.max, r=[gT], w=[ssc])
            VTS(ohT[:, c * 8:(c + 1) * 8], tg, ssc[:, c:c + 1], ALU.is_equal,
                r=[gT, ssc], w=[ohT])
            VSTT(g_v[:, c * 8:(c + 1) * 8], ohT[:, c * 8:(c + 1) * 8], -16.0,
                 iotaf[:, 0:8], ALU.mult, ALU.add, r=[ohT, iotaf], w=[g_v])
            VRED(idxc[:, c:c + 1], g_v[:, c * 8:(c + 1) * 8], ALU.min,
                 r=[g_v], w=[idxc])
            VTS(ohT[:, c * 8:(c + 1) * 8], g_v[:, c * 8:(c + 1) * 8],
                idxc[:, c:c + 1], ALU.is_equal, r=[g_v, idxc], w=[ohT])
        for batch in range(8):
            for j in range(4):
                c = batch * 4 + j
                TRANS(pTR[0:8, j * 128:(j + 1) * 128], ohT[:, c * 8:(c + 1) * 8],
                      r=[ohT], w=[pTR])
            ACT(oh1[0:8, batch * 512:(batch + 1) * 512], pTR[0:8, 0:512], AF.Copy,
                r=[pTR], w=[Sbf3])

        DMA("sync", AB[0:64, :], a_u6, r=[CD, AB], w=[AB])
        VTS(AB[0:64, :], AB[0:64, :], c1c2[:, 1:2], ALU.subtract,
            r=[AB, c1c2], w=[AB])
        for c in range(8):
            csl = slice(c * 512, (c + 1) * 512)
            MM(pSM[0:64, 0:512], rep8b[:, :], oh1[0:8, csl], True, True,
               r=[rep8b, Sbf3], w=[pSM])
            ACT(rp[3][0:64, 0:512], pSM[0:64, 0:512], AF.Copy, r=[pSM], w=[rp[3]])
            VTT(mtil[0:64, csl], AB[0:64, csl], rp[3][0:64, 0:512], ALU.mult,
                r=[AB, rp[3]], w=[Sbf1])

        def tap(name, src_ap, rhs):
            if name in tap_aps:
                DMA("gpsimd", tap_aps[name], src_ap, r=rhs)

        tap("t_gate", g_gate, [S2])
        tap("t_nsq", g_nsq, [S1])
        tap("t_ssc", ssc[:], [ssc])
        tap("t_idx", idxc[:], [idxc])
        tap("t_ax8", arr_x8, [S2])
        tap("t_brow", a_hi[0:16, :], [AB])
        tap("t_arrC", arr_C, [S1])
        tap("t_pack2", pack2[:], [pack2])
        tap("t_mtf", AB[0:64, :], [AB])
        tap("t_dr", a_lo, [AB])

        sch.barrier()
        esB.close()

        esC = ExitStack()
        SC = lambda n, sh, dtp=F32: _alloc(esC, "sbuf", n, sh, dtp)
        brep = [SC(f"brep{n}", [128, TQ], BF16) for n in range(NST)]
        crep = [SC(f"crep{n}", [128, TQ], BF16) for n in range(NST)]
        dneg = [SC(f"dneg{i}", [128, TQ]) for i in range(2)]
        xnl = [SC(f"xnl{i}", [128, TQ]) for i in range(2)]
        dxb = [SC(f"dxb{i}", [128, TQ], BF16) for i in range(2)]
        dAt = [SC(f"dAt{i}", [128, TQ]) for i in range(3)]
        ubt = [SC(f"ubt{i}", [128, TQ], BF16) for i in range(2)]
        hbt = [SC(f"hbt{i}", [128, TQ], BF16) for i in range(2)]
        xtt = [SC(f"xtt{i}", [128, TQ]) for i in range(2)]
        murq = SC("murq", [128, TQ])
        rrq = SC("rrq", [128, TQ])
        scrB = SC("scrB", [16, TQ], BF16)
        scrC = SC("scrC", [16, TQ], BF16)
        sl1 = SC("sl1", [128, TQ])
        rm2 = sl1
        sTq = SC("sTq", [128, NDC * TQ], BF16)
        xnt = [xtt[1], xtt[1]]
        ep1_ = SC("ep1_", [128, DL])
        rel_ = ep1_
        ep1 = [ep1_, ep1_]
        ep2_ = SC("ep2_", [128, DL])
        ep2 = [ep2_, ep2_]

        def bc_rep(dst, scr, scrh, n):
            for hh in range(2):
                pb = pSM if hh == 0 else pTR
                MM(pb[:, 0:512], sel16b[:, n * 128:(n + 1) * 128],
                   scr[0:16, hh * 512:hh * 512 + 512], True, True,
                   r=[sel16b, scrh], w=[pb])
                ACT(dst[:, hh * 512:hh * 512 + 512], pb[:, 0:512], AF.Copy,
                    r=[pb], w=[dst])

        def emit_epi_a(eq, tt):
            tpar = tt % 2
            gt = eq * 8 + tt
            rowsl = slice(eq * TQ + tt * 128, eq * TQ + tt * 128 + 128)
            DMA("sync", xnt[tpar][:], xN[rowsl, :], w=[xnt[tpar]])
            for hh in range(2):
                hsl = slice(hh * 512, hh * 512 + 512)
                MM(pDP[:, hsl], mtil[0:64, gt * 128:(gt + 1) * 128],
                   dgtb[:, hsl], True, False, r=[Sbf1, dgtb], w=[pDP])
                MM(pDP[:, hsl], oh1[0:8, gt * 128:(gt + 1) * 128],
                   bwlb[:, hsl], False, True, r=[Sbf3, bwlb], w=[pDP])
            VTS(ep1[tpar][:], xnt[tpar][:], statc[:, 2 * gt:2 * gt + 1],
                ALU.mult, r=[xnt[tpar], statc], w=[ep1[tpar]],
                s2=statc[:, 2 * gt + 1:2 * gt + 2], op1=ALU.subtract)
            ACT(ep2[tpar][:, 0:512], pDP[:, 0:512], AF.Copy,
                r=[pDP], w=[ep2[tpar]])
            ACT(ep2[tpar][:, 512:1024], pDP[:, 512:1024], AF.Copy,
                r=[pDP], w=[ep2[tpar]])
            VTT(ep2[tpar][:], ep2[tpar][:], ep1[tpar][:], ALU.subtract,
                r=[ep2[tpar], ep1[tpar]], w=[ep2[tpar]])
            VSTT(ep2[tpar][:], ep2[tpar][:], ssc[:, gt:gt + 1],
                 ep1[tpar][:], ALU.mult, ALU.add,
                 r=[ep2[tpar], ssc, ep1[tpar]], w=[ep2[tpar]])
            ACT(ep2[tpar][:], ep2[tpar][:], AF.Silu,
                r=[ep2[tpar]], w=[ep2[tpar]])

        def emit_epi_b(eq, tt):
            tpar = tt % 2
            rowsl = slice(eq * TQ + tt * 128, eq * TQ + tt * 128 + 128)
            sview = sTq[:].rearrange("p (d t) -> p d t", d=NDC)[
                :, :, tt * 128:(tt + 1) * 128]
            gview = ep2[tpar][:].rearrange("p (d t) -> p d t", d=NDC)
            oview = ep1[tpar][:].rearrange("p (d t) -> p d t", d=NDC)
            sch.op("vector", lambda e, ov=oview, sv=sview, gv=gview:
                   e.tensor_tensor(out=ov, in0=sv, in1=gv, op=ALU.mult),
                   bl([sTq, ep2[tpar]]), bl([ep1[tpar]]))
            VTT(ep1[tpar][:], ep1[tpar][:], xnt[tpar][:], ALU.add,
                r=[ep1[tpar], xnt[tpar]], w=[ep1[tpar]])
            DMA("sync", out[rowsl, :], ep1[tpar][:], r=[ep1[tpar]])

        def emit_epi(eq, tt):
            emit_epi_a(eq, tt)
            emit_epi_b(eq, tt)

        def qpro_0(tq):
            bcast2([rrq, murq], tq * TQ)
            ln_chain(rrq, murq, rel_)

        def qpro_1(tq):
            for j in range(8):
                TRANS(pTR[0:128, j * 2:j * 2 + 1],
                      rrq[0:1, j * 128:(j + 1) * 128], r=[rrq], w=[pTR])
                TRANS(pTR[0:128, j * 2 + 1:j * 2 + 2],
                      murq[0:1, j * 128:(j + 1) * 128], r=[murq], w=[pTR])
            ACT(statc[:, tq * 16:tq * 16 + 16], pTR[:, 0:16], AF.Copy,
                r=[pTR], w=[statc])
            tsl = slice(tq * TQ, tq * TQ + TQ)
            DMA("sync", scrB[:], bnegb[0:16, tsl], r=[Sbf1], w=[scrB])
            DMA("sync", scrC[:], crowb[0:16, tsl], r=[Sbf3], w=[scrC])

        pending_epi = None
        pending_tail = None
        for q in range(NQ):
            qoff = q * TQ
            qsl = slice(qoff, qoff + TQ)
            if q == 0:
                qpro_0(0)
                qpro_1(0)
                for n in range(2):
                    bc_rep(brep[n][:], scrB[:], scrB, n)
                    bc_rep(crep[n][:], scrC[:], scrC, n)

            def dpro_a0(d, toff):
                MM(pDP[:, 0:512], dtwb[:, d * 128:(d + 1) * 128],
                   xdbr[0:64, toff:toff + 512], True, True, r=[dtwb, Sbf2], w=[pDP])
                MM(pDP[:, 512:1024], dtwb[:, d * 128:(d + 1) * 128],
                   xdbr[0:64, toff + 512:toff + 1024], True, True,
                   r=[dtwb, Sbf2], w=[pDP])

            def dpro_a1(d):
                dp = d % 2
                ACT(dneg[dp][:], pDP[:, :], AF.Abs, r=[pDP, posb],
                    w=[dneg[dp]], bias=posb[:, d:d + 1])

            def dpro_a2(d):
                dp = d % 2
                ACT(dneg[dp][:], dneg[dp][:], AF.Exp, scale=-1.0,
                    r=[dneg[dp]], w=[dneg[dp]])

            def dpro_a3(d):
                ACT(rel_[:], pDP[:, :], AF.Relu, r=[pDP, posb],
                    w=[rel_], bias=posb[:, d:d + 1])

            def dpro_a4(d):
                dp = d % 2
                ACT(dneg[dp][:], dneg[dp][:], AF.Ln, bias=1.0,
                    r=[dneg[dp]], w=[dneg[dp]])
                sch.op("gpsimd", lambda e, o=dneg[dp][:], a=rel_[:]:
                       e.tensor_tensor(out=o, in0=o, in1=a, op=ALU.add),
                       bl([dneg[dp], rel_]), bl([dneg[dp]]))

            def emit_dpro_a(d, toff):
                dpro_a0(d, toff)
                dpro_a1(d)
                dpro_a2(d)
                dpro_a3(d)
                dpro_a4(d)

            def dpro_b_dma(d, toff):
                dp = d % 2
                DMA("scalar", xtt[dp][:], xT[d * 128:(d + 1) * 128,
                                             toff:toff + TQ],
                    w=[xtt[dp]])

            def emit_dpro_b(d, toff, with_dma=True):
                dp = d % 2
                if with_dma:
                    dpro_b_dma(d, toff)
                VTT(xnl[dp][:], xtt[dp][:], rrq[:], ALU.mult,
                    r=[xtt[dp], rrq], w=[xnl[dp]])
                VTT(xnl[dp][:], xnl[dp][:], murq[:], ALU.subtract,
                    r=[xnl[dp], murq], w=[xnl[dp]])
                VTT(dxb[dp][:], dneg[dp][:], xnl[dp][:], ALU.mult,
                    r=[dneg[dp], xnl[dp]], w=[dxb[dp]])

            def emit_dtail_a(d_, pyb_, dp_):
                ACT(sl1[:, 0:512], pyb_[:, 0:512], AF.Copy, r=[pyb_], w=[sl1])
                ACT(sl1[:, 512:1024], pyb_[:, 512:1024], AF.Copy,
                    r=[pyb_], w=[sl1])
                VSTT(sl1[:], xnl[dp_][:], dvecf[:, d_:d_ + 1], sl1[:],
                     ALU.mult, ALU.add, r=[xnl[dp_], dvecf, sl1], w=[sl1])

            def emit_dtail_b(d_, pyb_, dp_, g4):
                for j in range(4):
                    tt = g4 * 4 + j
                    TRANS(pTR[:, j * 128:(j + 1) * 128],
                          sl1[:, tt * 128:(tt + 1) * 128],
                          r=[sl1], w=[pTR])
                ACT(sTq[:, d_ * TQ + g4 * 512:d_ * TQ + g4 * 512 + 512],
                    pTR[:, 0:512], AF.Copy, r=[pTR], w=[sTq])

            def emit_dtail_trans(d_, pyb_, dp_):
                for j in range(4):
                    TRANS(pTR[:, j * 128:(j + 1) * 128],
                          sl1[:, j * 128:(j + 1) * 128], r=[sl1], w=[pTR])
                for j in range(4):
                    TRANS(pSM[:, j * 128:(j + 1) * 128],
                          sl1[:, (4 + j) * 128:(5 + j) * 128],
                          r=[sl1], w=[pSM])

            def emit_dtail_copy(d_, pyb_, dp_, g4):
                pb = pTR if g4 == 0 else pSM
                ACT(sTq[:, d_ * TQ + g4 * 512:d_ * TQ + g4 * 512 + 512],
                    pb[:, 0:512], AF.Copy, r=[pb], w=[sTq])

            if q == 0:
                emit_dpro_a(0, qoff)
                emit_dpro_b(0, qoff)
            for d in range(NDC):
                dp = d % 2
                pyb = pY0 if d % 2 == 0 else pY1
                for n in range(NST):
                    npar = n % 2
                    col = d * NST + n
                    ai = col % 3
                    if n == 14 and d >= 1 and pending_tail is not None:
                        emit_dtail_trans(*pending_tail)
                    VTT(ubt[npar][:], dxb[dp][:], brep[n][:], ALU.mult,
                        r=[dxb[dp], brep[n]], w=[ubt[npar]])
                    if not (n < 3 and (d > 0 or q > 0)):
                        ACT(dAt[ai][:], dneg[dp][:], AF.Exp,
                            r=[dneg[dp], anegs], w=[dAt[ai]],
                            scale=anegs[:, col:col + 1])
                    sch.op("vector", lambda e, o=hbt[npar][:], a=dAt[ai][:],
                           b=ubt[npar][:], ini=st_[:, col:col + 1]:
                           e.tensor_tensor_scan(out=o, data0=a, data1=b,
                                                initial=ini, op0=ALU.mult,
                                                op1=ALU.add),
                           bl([dAt[ai], ubt[npar]]) + [stcol[col]],
                           bl([hbt[npar]]))
                    sch.op("gpsimd", lambda e, o=st_[:, col:col + 1],
                           i=hbt[npar][:, TQ - 1:TQ]: e.tensor_copy(out=o, in_=i),
                           bl([hbt[npar]]), [stcol[col]])
                    VTT(ubt[npar][:], hbt[npar][:], crep[n][:], ALU.mult,
                        r=[hbt[npar], crep[n]], w=[ubt[npar]])
                    MM(pyb[:, 0:512], idenb[:, :], ubt[npar][:, 0:512],
                       n == 0, n == NST - 1, r=[idenb, ubt[npar]], w=[pyb])
                    MM(pyb[:, 512:1024], idenb[:, :], ubt[npar][:, 512:1024],
                       n == 0, n == NST - 1, r=[idenb, ubt[npar]], w=[pyb])
                    if n == 2 and pending_tail is not None:
                        emit_dtail_a(*pending_tail)
                    if d == 0:
                        if n == 5 and pending_tail is not None:
                            emit_dtail_b(*pending_tail, 0)
                        if n == 6 and pending_tail is not None:
                            emit_dtail_b(*pending_tail, 1)
                            pending_tail = None
                    else:
                        if n == 14 and pending_tail is not None:
                            emit_dtail_copy(*pending_tail, 0)
                            emit_dtail_copy(*pending_tail, 1)
                            pending_tail = None
                    if q == 0 and d == 0 and n + 2 < NST:
                        bc_rep(brep[n + 2][:], scrB[:], scrB, n + 2)
                        bc_rep(crep[n + 2][:], scrC[:], scrC, n + 2)
                    if d == NDC - 1 and q + 1 < NQ:
                        if n == 0:
                            qpro_0(q + 1)
                        elif n == 2:
                            qpro_1(q + 1)
                        elif n >= 3:
                            bc_rep(brep[n - 3][:], scrB[:], scrB, n - 3)
                            bc_rep(crep[n - 3][:], scrC[:], scrC, n - 3)
                    if (pending_epi is not None and d <= 1
                            and n in (7, 9, 11, 13)):
                        emit_epi_b(pending_epi, d * 4 + (n - 7) // 2)
                    if d + 1 < NDC:
                        td, toff = d + 1, qoff
                    elif q + 1 < NQ:
                        td, toff = 0, qoff + TQ
                    else:
                        td = None
                    if td is not None:
                        if n == 2:
                            dpro_a0(td, toff)
                            if d >= 1:
                                dpro_b_dma(td, toff)
                        elif n == 3:
                            dpro_a1(td)
                        elif n == 4:
                            dpro_a2(td)
                        elif n == 5:
                            dpro_a3(td)
                        elif n == 6:
                            dpro_a4(td)
                        elif n == 9:
                            emit_dpro_b(td, toff, with_dma=(d == 0))
                    if (pending_epi is not None and d <= 1
                            and n in (6, 8, 10, 12)):
                        emit_epi_a(pending_epi, d * 4 + (n - 6) // 2)

                if d + 1 < NDC:
                    ndp, nbase = (d + 1) % 2, (d + 1) * NST
                elif q + 1 < NQ:
                    ndp, nbase = 0, 0
                else:
                    ndp = None
                if ndp is not None:
                    for nn in range(3):
                        ncol = nbase + nn
                        ACT(dAt[ncol % 3][:], dneg[ndp][:], AF.Exp,
                            r=[dneg[ndp], anegs], w=[dAt[ncol % 3]],
                            scale=anegs[:, ncol:ncol + 1])
                if d == NDC - 1 and q + 1 < NQ:
                    for nn in (13, 14, 15):
                        bc_rep(brep[nn][:], scrB[:], scrB, nn)
                        bc_rep(crep[nn][:], scrC[:], scrC, nn)
                pending_tail = (d, pyb, dp)

            if ("t_sT%d" % q) in tap_aps:
                tap("t_sT%d" % q, sTq[:], [sTq])
            pending_epi = q

        if pending_tail is not None:
            emit_dtail_a(*pending_tail)
            emit_dtail_b(*pending_tail, 0)
            emit_dtail_b(*pending_tail, 1)
            pending_tail = None
        xnt[0] = xtt[0]
        for tt in range(8):
            emit_epi(NQ - 1, tt)

        allw = [(s, sch.count[s]) for s in Sched.SEMS if sch.count[s]]
        sch.op("gpsimd", lambda e: e.nop(), [], [], extra=allw)

        esC.close()
        sch.build(block)

    es.close()
    return nc, tap_aps


def _chunked(a):
    k = a.shape[0] // 128
    return np.ascontiguousarray(
        a.reshape(k, 128, a.shape[1]).transpose(1, 0, 2).reshape(128, -1))


def _stage_inputs(inputs):
    f = np.float32
    x = np.asarray(inputs["L0_attn_x"], dtype=f)
    xpw = np.asarray(inputs["L0_attn_x_proj_w"], dtype=f)
    dtw_w = np.asarray(inputs["L0_attn_dt_proj_w"], dtype=f)
    dtb_v = np.asarray(inputs["L0_attn_dt_proj_b"], dtype=f)
    alog_w = np.asarray(inputs["L0_attn_A_log"], dtype=f)
    dvec_v = np.asarray(inputs["L0_attn_D"], dtype=f)
    selw = np.asarray(inputs["L0_ffn_select_weight"], dtype=f)
    md = np.asarray(inputs["L0_ffn_select_mean_diff"], dtype=f)
    ictr = np.asarray(inputs["L0_ffn_imp_center"], dtype=f)
    islp = np.asarray(inputs["L0_ffn_imp_slope"], dtype=f)
    lup = np.asarray(inputs["L0_ffn_lora_update"], dtype=f)
    ldg = np.asarray(inputs["L0_ffn_lora_degen"], dtype=f)
    bw = np.asarray(inputs["L0_ffn_bias_w"], dtype=f)
    dbw = np.asarray(inputs["L0_ffn_debias_w"], dtype=f)

    wsel_flat = selw.transpose(1, 0, 2).reshape(D, 64)
    u_flat = lup.transpose(1, 0, 2).reshape(D, 64)
    dg_flat = ldg.transpose(0, 2, 1).reshape(64, D)
    g1 = np.concatenate([xpw.T, md.T, np.ones((D, 1), f),
                         np.zeros((D, 23), f)], axis=1)
    g2 = np.concatenate([wsel_flat, u_flat], axis=1)

    iden = np.eye(128, dtype=f)
    iota16 = np.tile(np.arange(8, dtype=f) + 16.0, (128, 1))
    onesc = np.ones((128, 1), f)
    sel2 = np.zeros((2, 256), f)
    sel2[0, 0:128] = 1.0
    sel2[1, 128:256] = 1.0
    rep8 = np.zeros((8, 64), f)
    for n in range(8):
        rep8[n, n * 8:(n + 1) * 8] = 1.0
    sels = np.zeros((64, 8), f)
    for n in range(8):
        sels[n * 8:(n + 1) * 8, n] = 1.0
    sel16 = np.zeros((16, 16 * 128), f)
    for n in range(16):
        sel16[n, n * 128:(n + 1) * 128] = 1.0

    in_maps = []
    for c in range(NCORES):
        b, half = c // 2, c % 2
        lo = half * DL
        perm = np.concatenate([np.arange(lo, lo + DL),
                               np.arange((1 - half) * DL, (1 - half) * DL + DL)])
        xb = x[b]
        in_maps.append({
            "xT": np.ascontiguousarray(xb.T[perm]),
            "xN": np.ascontiguousarray(xb[:, lo:lo + DL]),
            "wg1": _chunked(g1[perm]),
            "wg2": _chunked(g2[perm]),
            "dtw": np.ascontiguousarray(dtw_w.T[:, lo:lo + DL]),
            "dgt": np.ascontiguousarray(dg_flat[:, lo:lo + DL]),
            "bwl": np.ascontiguousarray(bw[:, lo:lo + DL]),
            "mdTd": _chunked(md.T[perm]),
            "dbTd": _chunked(dbw.T[perm]),
            "alog": _chunked(alog_w[lo:lo + DL]),
            "dtb": _chunked(dtb_v[lo:lo + DL, None]),
            "dvec": _chunked(dvec_v[lo:lo + DL, None]),
            "slp": np.ascontiguousarray(islp[:, None]),
            "ctr": np.ascontiguousarray(ictr[:, None]),
            "iden": iden, "iota16": iota16, "onesc": onesc, "sel2": sel2,
            "rep8": rep8, "sels": sels, "sel16": sel16,
        })
    return in_maps


def kernel(**inputs):
    in_maps = _stage_inputs(inputs)
    nc, _ = build_nc()
    res = run_bass_kernel_spmd(nc, in_maps, core_ids=list(range(NCORES)))
    outp = np.empty((4, T, D), dtype=np.float32)
    for c in range(NCORES):
        b, half = c // 2, c % 2
        lo = half * DL
        outp[b, :, lo:lo + DL] = np.asarray(res.results[c]["out"])
    return outp


if __name__ == "__main__":
    nc, _ = build_nc()
    print("built ok")


# version 34 (final)
# speedup vs baseline: 1.0077x; 1.0077x over previous
import numpy as np
from contextlib import ExitStack

import concourse.bass as bass
import concourse.mybir as mybir
from concourse.bass_utils import run_bass_kernel_spmd

F32 = mybir.dt.float32
BF16 = mybir.dt.bfloat16
AF = mybir.ActivationFunctionType
ALU = mybir.AluOpType
AX = mybir.AxisListType

D = 2048
DL = 1024
T = 4096
NST = 16
RK = 64
TQ = 1024
NQ = T // TQ
NDC = DL // 128
NKC = D // 128
NTT = T // 128
EPS = 1e-5
NCORES = 8


class Buf:
    __slots__ = ("writer", "readers")

    def __init__(self):
        self.writer = None
        self.readers = []


class Sched:
    ENGS = ("sync", "scalar", "vector", "tensor", "gpsimd")
    SEMS = [e for e in ENGS] + [e + "&d" for e in ENGS]

    def __init__(self, nc):
        self.nc = nc
        self.items = {e: [] for e in self.ENGS}
        self.count = {s: 0 for s in self.SEMS}
        self.sems = {}

    def make_sems(self, es):
        for s in self.SEMS:
            self.sems[s] = es.enter_context(
                self.nc.semaphore("sem_" + s.replace("&", "_")))

    def op(self, eng, fn, reads=(), writes=(), dma=False, extra=()):
        waits = list(extra)
        for b in reads:
            if b.writer is not None:
                waits.append(b.writer)
        for b in writes:
            if b.writer is not None:
                waits.append(b.writer)
            waits.extend(b.readers)
        sem = eng + "&d" if dma else eng
        inc = 16 if dma else 1
        tok = (sem, self.count[sem] + inc)
        best = {}
        for (ws, wv) in waits:
            if eng == "tensor" and ws == "tensor":
                continue
            if ws.endswith("&d"):
                wv = self.count[ws]
            if wv > best.get(ws, -1):
                best[ws] = wv
        self.items[eng].append((tuple(best.items()), fn, inc, sem))
        self.count[sem] += inc
        for b in reads:
            b.readers.append(tok)
        for b in writes:
            b.writer = tok
            b.readers = []
        return tok

    def barrier(self):
        snap = [(s, self.count[s]) for s in self.SEMS if self.count[s]]
        for e in self.ENGS:
            self.op(e, lambda eng: eng.nop(), [], [], extra=snap)

    def build(self, block):
        for eng in self.ENGS:
            items = self.items[eng]
            if not items:
                continue

            def make(items):
                def section(engine):
                    waited = {s: 0 for s in self.SEMS}
                    for waits, fn, inc, sem in items:
                        for (ws, wv) in waits:
                            if wv > waited[ws]:
                                engine.wait_ge(self.sems[ws], wv)
                                waited[ws] = wv
                        fn(engine).then_inc(self.sems[sem], inc)
                return section

            getattr(block, eng)(make(items))


def build_nc(taps=()):
    nc = bass.Bass("TRN2", target_bir_lowering=False, debug=False)
    dt = nc.dram_tensor

    xT = dt("xT", [D, T], F32, kind="ExternalInput").ap()
    xN = dt("xN", [T, DL], F32, kind="ExternalInput").ap()
    wg1 = dt("wg1", [128, NKC * 128], F32, kind="ExternalInput").ap()
    wg2 = dt("wg2", [128, NKC * 128], F32, kind="ExternalInput").ap()
    dtw = dt("dtw", [RK, DL], F32, kind="ExternalInput").ap()
    dgt = dt("dgt", [64, DL], F32, kind="ExternalInput").ap()
    bwl = dt("bwl", [8, DL], F32, kind="ExternalInput").ap()
    mdTd = dt("mdTd", [128, NKC * 8], F32, kind="ExternalInput").ap()
    dbTd = dt("dbTd", [128, NKC * 8], F32, kind="ExternalInput").ap()
    alog = dt("alog", [128, NDC * NST], F32, kind="ExternalInput").ap()
    dtb = dt("dtb", [128, NDC], F32, kind="ExternalInput").ap()
    dvec = dt("dvec", [128, NDC], F32, kind="ExternalInput").ap()
    slp = dt("slp", [8, 1], F32, kind="ExternalInput").ap()
    ctr = dt("ctr", [8, 1], F32, kind="ExternalInput").ap()
    iden = dt("iden", [128, 128], F32, kind="ExternalInput").ap()
    iota16 = dt("iota16", [128, 8], F32, kind="ExternalInput").ap()
    onesc = dt("onesc", [128, 1], F32, kind="ExternalInput").ap()
    sel2 = dt("sel2", [2, 256], F32, kind="ExternalInput").ap()
    rep8 = dt("rep8", [8, 64], F32, kind="ExternalInput").ap()
    sel16 = dt("sel16", [16, 16 * 128], F32, kind="ExternalInput").ap()
    sels = dt("sels", [64, 8], F32, kind="ExternalInput").ap()

    out = dt("out", [T, DL], F32, kind="ExternalOutput").ap()
    tap_aps = {}
    for tspec in taps:
        name, shape = tspec[0], tspec[1]
        tdt = BF16 if (len(tspec) > 2 and tspec[2] == "bf16") else F32
        tap_aps[name] = dt(name, list(shape), tdt, kind="ExternalOutput").ap()

    es = ExitStack()
    BUFS = {}
    sch = Sched(nc)

    def _alloc(stack, kind, name, shape, dtype=F32):
        if kind == "sbuf":
            h = stack.enter_context(nc.sbuf_tensor(name, shape, dtype))
        else:
            h = stack.enter_context(nc.psum_tensor(name, shape, dtype))
        BUFS[name] = Buf()
        return h

    def S(name, shape, dtype=F32):
        return _alloc(es, "sbuf", name, shape, dtype)

    def PSUM(name, shape):
        return _alloc(es, "psum", name, shape, F32)

    def bl(hs):
        return [BUFS[h.name] for h in hs]

    sel16b = S("sel16b", [16, 16 * 128], BF16)
    dtwb = S("dtwb", [RK, DL], BF16)
    dgtb = S("dgtb", [64, DL], BF16)
    bwlb = S("bwlb", [8, DL], BF16)
    anegs = S("anegs", [128, NDC * NST])
    negb = S("negb", [128, NDC])
    posb = S("posb", [128, NDC])
    dvecf = S("dvecf", [128, NDC])
    slpf = S("slpf", [8, 1]); ctrf = S("ctrf", [8, 1]); nslpc = S("nslpc", [8, 1])
    idenf = S("idenf", [128, 128]); idenb = S("idenb", [128, 128], BF16)
    iotaf = S("iotaf", [128, 8])
    onescf = S("onescf", [128, 1]); onescb = S("onescb", [128, 1], BF16)
    nhalfc = S("nhalfc", [128, 1])
    sel2f = S("sel2f", [2, 256])
    rep8b = S("rep8b", [8, 64], BF16)
    selsf = S("selsf", [64, 8])
    pack2 = S("pack2", [2, T])
    Sbf1 = S("Sbf1", [128, T], BF16)
    mtil = Sbf1[0:64, :]
    bnegb = Sbf1[64:80, :]
    Sbf2 = S("Sbf2", [128, T], BF16)
    xdbr = Sbf2[0:64, :]
    Sbf3 = S("Sbf3", [128, T], BF16)
    oh1 = Sbf3[0:8, :]
    crowb = Sbf3[32:48, :]
    statc = S("statc", [128, 2 * NTT])
    stcol = [Buf() for _ in range(NDC * NST)]
    ssc = S("ssc", [128, NTT])
    idxc = S("idxc", [128, NTT])
    st_ = S("st_", [128, NDC * NST])

    pY0 = PSUM("pY0", [128, 1024])
    pY1 = PSUM("pY1", [128, 1024])
    pDP = PSUM("pDP", [128, 1024])
    pTR = PSUM("pTR", [128, 512])
    pSM = PSUM("pSM", [128, 512])

    with nc.Block() as block:
        sch.make_sems(es)

        def DMA(q, out_ap, in_ap, r=(), w=()):
            return sch.op(q, lambda e: e.dma_start(out=out_ap, in_=in_ap),
                          bl(r), bl(w), dma=True)

        def ACT(out_ap, in_ap, func, r=(), w=(), bias=None, scale=None):
            kw = {}
            if bias is not None:
                kw["bias"] = bias
            if scale is not None:
                kw["scale"] = scale
            return sch.op("scalar", lambda e: e.activation(
                out=out_ap, in_=in_ap, func=func, **kw), bl(r), bl(w))

        def VTT(out_ap, a, b_, alu, r=(), w=()):
            return sch.op("vector", lambda e: e.tensor_tensor(
                out=out_ap, in0=a, in1=b_, op=alu), bl(r), bl(w))

        def VTS(out_ap, in_ap, s1, op0, r=(), w=(), s2=None, op1=None):
            if op1 is None:
                return sch.op("vector", lambda e: e.tensor_scalar(
                    out=out_ap, in0=in_ap, scalar1=s1, scalar2=None, op0=op0),
                    bl(r), bl(w))
            return sch.op("vector", lambda e: e.tensor_scalar(
                out=out_ap, in0=in_ap, scalar1=s1, scalar2=s2, op0=op0, op1=op1),
                bl(r), bl(w))

        def VSTT(out_ap, in0, s, in1, op0, op1, r=(), w=()):
            return sch.op("vector", lambda e: e.scalar_tensor_tensor(
                out=out_ap, in0=in0, scalar=s, in1=in1, op0=op0, op1=op1),
                bl(r), bl(w))

        def VRED(out_ap, in_ap, alu, r=(), w=()):
            return sch.op("vector", lambda e: e.tensor_reduce(
                out=out_ap, in_=in_ap, axis=AX.X, op=alu), bl(r), bl(w))

        def VCOPY(out_ap, in_ap, r=(), w=()):
            return sch.op("vector", lambda e: e.tensor_copy(
                out=out_ap, in_=in_ap), bl(r), bl(w))

        def VREC(out_ap, in_ap, r=(), w=()):
            return sch.op("vector", lambda e: e.reciprocal(
                out=out_ap, in_=in_ap), bl(r), bl(w))

        def VSCAN(out_ap, d0, d1, init, r=(), w=()):
            return sch.op("vector", lambda e: e.tensor_tensor_scan(
                out=out_ap, data0=d0, data1=d1, initial=init,
                op0=ALU.mult, op1=ALU.add), bl(r), bl(w))

        def MM(out_ap, lhsT, rhs, start, stop, r=(), w=()):
            return sch.op("tensor", lambda e: e.matmul(
                out=out_ap, lhsT=lhsT, rhs=rhs, start=start, stop=stop),
                bl(r), bl(w))

        def TRANS(out_ap, in_ap, r=(), w=()):
            p = in_ap.shape[0]
            return sch.op("tensor", lambda e: e.transpose(
                out=out_ap, in_=in_ap, identity=idenf[0:p, 0:p]),
                bl(r) + [BUFS["idenf"]], bl(w))

        def MEMSET(out_ap, val, w=()):
            return sch.op("vector", lambda e: e.memset(out_ap, val), [], bl(w))

        def ln_chain(t0, t1, tmp, mu_out=None):
            VTS(t1[:], t1[:], 1.0 / D, ALU.mult, r=[t1], w=[t1])
            if mu_out is not None:
                VCOPY(mu_out[:], t1[:], r=[t1], w=[mu_out])
            VTT(tmp[:], t1[:], t1[:], ALU.mult, r=[t1], w=[tmp])
            VSTT(t0[:], t0[:], 1.0 / D, tmp[:], ALU.mult, ALU.subtract,
                 r=[t0, tmp], w=[t0])
            VTS(t0[:], t0[:], EPS, ALU.add, r=[t0], w=[t0])
            ACT(t0[:], t0[:], AF.Ln, r=[t0], w=[t0])
            ACT(t0[:], t0[:], AF.Exp, scale=nhalfc[:], r=[t0, nhalfc], w=[t0])
            VTT(t1[:], t1[:], t0[:], ALU.mult, r=[t1, t0], w=[t1])

        def bcast2(dst, qoff):
            for j, d_ in reversed(list(enumerate(dst))):
                for hh in range(2):
                    ssl = slice(qoff + hh * 512, qoff + hh * 512 + 512)
                    pb = pTR if hh == 0 else pSM
                    MM(pb[:, 0:512], sel2f[:, j * 128:(j + 1) * 128],
                       pack2[0:2, ssl], True, True, r=[sel2f, pack2], w=[pb])
                    ACT(d_[:, hh * 512:hh * 512 + 512], pb[:, 0:512], AF.Copy,
                        r=[pb], w=[d_])

        esA = ExitStack()
        SA = lambda n, sh, dtp=F32: _alloc(esA, "sbuf", n, sh, dtp)
        sel16f = SA("sel16f", [16, 16 * 128])
        dtwf = SA("dtwf", [RK, DL])
        dgtf = SA("dgtf", [64, DL])
        bwlf = SA("bwlf", [8, DL])
        alogf = SA("alogf", [128, NDC * NST])
        dtbf = SA("dtbf", [128, NDC])
        rep8f = SA("rep8f", [8, 64])

        DMA("sync", sel16f[:], sel16, w=[sel16f])
        ACT(sel16b[:], sel16f[:], AF.Copy, r=[sel16f], w=[sel16b])
        DMA("sync", dtwf[:], dtw, w=[dtwf])
        DMA("sync", dgtf[:], dgt, w=[dgtf])
        DMA("sync", bwlf[:], bwl, w=[bwlf])
        DMA("sync", alogf[:], alog, w=[alogf])
        DMA("sync", dtbf[:], dtb, w=[dtbf])
        DMA("sync", dvecf[:], dvec, w=[dvecf])
        DMA("sync", slpf[:], slp, w=[slpf])
        DMA("sync", ctrf[:], ctr, w=[ctrf])
        DMA("sync", idenf[:], iden, w=[idenf])
        DMA("sync", iotaf[:], iota16, w=[iotaf])
        DMA("sync", onescf[:], onesc, w=[onescf])
        DMA("sync", sel2f[:], sel2, w=[sel2f])
        DMA("sync", rep8f[:], rep8, w=[rep8f])
        DMA("sync", selsf[:], sels, w=[selsf])

        ACT(dtwb[:], dtwf[:], AF.Copy, r=[dtwf], w=[dtwb])
        ACT(dgtb[:], dgtf[:], AF.Copy, r=[dgtf], w=[dgtb])
        ACT(bwlb[:], bwlf[:], AF.Copy, r=[bwlf], w=[bwlb])
        ACT(anegs[:], alogf[:], AF.Exp, r=[alogf], w=[anegs])
        VTS(anegs[:], anegs[:], -1.0, ALU.mult, r=[anegs], w=[anegs])
        ACT(posb[:], dtbf[:], AF.Copy, r=[dtbf], w=[posb])
        ACT(onescb[:], onescf[:], AF.Copy, r=[onescf], w=[onescb])
        ACT(idenb[:], idenf[:], AF.Copy, r=[idenf], w=[idenb])
        ACT(rep8b[:], rep8f[:], AF.Copy, r=[rep8f], w=[rep8b])
        VTS(negb[:], dtbf[:], -1.0, ALU.mult, r=[dtbf], w=[negb])
        VTS(nhalfc[:], onescf[:], -0.5, ALU.mult, r=[onescf], w=[nhalfc])
        VTT(nslpc[:], slpf[:], ctrf[:], ALU.mult, r=[slpf, ctrf], w=[nslpc])
        VTS(nslpc[:], nslpc[:], -1.0, ALU.mult, r=[nslpc], w=[nslpc])
        sch.op("vector", lambda e: e.memset(st_[:], 0.0), [],
               [BUFS["st_"]] + stcol)

        sch.barrier()
        esA.close()

        esB = ExitStack()
        SB = lambda n, sh, dtp=F32: _alloc(esB, "sbuf", n, sh, dtp)
        wg1f = SB("wg1f", [128, NKC * 128])
        wg2f = SB("wg2f", [128, NKC * 128])
        wg1b = SB("wg1b", [128, NKC * 128], BF16)
        wg2b = SB("wg2b", [128, NKC * 128], BF16)
        DMA("sync", wg1f[:], wg1, w=[wg1f])
        DMA("sync", wg2f[:], wg2, w=[wg2f])
        ACT(wg1b[:], wg1f[:], AF.Copy, r=[wg1f], w=[wg1b])
        ACT(wg2b[:], wg2f[:], AF.Copy, r=[wg2f], w=[wg2b])
        mdTf = SB("mdTf", [128, NKC * 8])
        dbTf = SB("dbTf", [128, NKC * 8])
        mdTb = SB("mdTb", [128, NKC * 8], BF16)
        dbTb = SB("dbTb", [128, NKC * 8], BF16)
        mdsqb = SB("mdsqb", [128, NKC * 8], BF16)
        DMA("sync", mdTf[:], mdTd, w=[mdTf])
        DMA("sync", dbTf[:], dbTd, w=[dbTf])
        ACT(mdTb[:], mdTf[:], AF.Copy, r=[mdTf], w=[mdTb])
        ACT(dbTb[:], dbTf[:], AF.Copy, r=[dbTf], w=[dbTb])
        ACT(mdsqb[:], mdTf[:], AF.Square, r=[mdTf], w=[mdsqb])
        AB = SB("AB", [128, T])
        CD = SB("CD", [128, T])
        S1 = SB("S1", [128, T])
        S2 = SB("S2", [128, T])
        rp = [SB(f"rp{i}", [128, TQ]) for i in range(4)]
        xin = [SB(f"xin{i}", [128, TQ]) for i in range(3)]
        xbf = [SB(f"xbf{i}", [128, TQ], BF16) for i in range(3)]
        xsq = [SB(f"xsq{i}", [128, TQ], BF16) for i in range(3)]
        gT = SB("gT", [128, 8 * NTT])
        ohT = SB("ohT", [128, 8 * NTT])
        g_v = SB("g_v", [128, 8 * NTT])
        cross = SB("cross", [64, 16])
        c1c2 = SB("c1c2", [64, 2])
        ws_g1 = SB("ws_g1", [128, 1]); ws_g2 = SB("ws_g2", [128, 1])
        nws_g1 = SB("nws_g1", [128, 1]); nws_g2 = SB("nws_g2", [128, 1])
        nws2 = SB("nws2", [128, 1])
        nws3 = SB("nws3", [128, 1])
        mdsqs = SB("mdsqs", [8, 1])

        a_lo = AB[0:64, :]
        a_hi = AB[64:128, :]
        a_ws = CD[0:64, :]
        a_u6 = CD[64:128, :]
        arr_C = S1[32:48, :]
        g_nsq = S1[0:8, :]
        arr_x8 = S2[32:40, :]
        arr_x8c = S2[0:8, :]
        g_gate = S2[0:8, :]
        g_sc = AB[0:8, :]

        for tcg in range(4):
            for k in range(NKC):
                gi = tcg * NKC + k
                par = gi % 3
                DMA(("sync", "scalar", "gpsimd")[par], xin[par][:],
                    xT[k * 128:(k + 1) * 128, tcg * 1024:(tcg + 1) * 1024],
                    w=[xin[par]])
                ACT(xbf[par][:], xin[par][:], AF.Copy,
                    r=[xin[par]], w=[xbf[par]])
                VTT(xsq[par][:], xin[par][:], xin[par][:], ALU.mult,
                    r=[xin[par]], w=[xsq[par]])
                st0, sp0 = (k == 0), (k == NKC - 1)
                for hh in range(2):
                    hsl = slice(hh * 512, hh * 512 + 512)
                    MM(pY0[:, hsl], wg1b[:, k * 128:(k + 1) * 128],
                       xbf[par][:, hsl], st0, sp0, r=[wg1b, xbf[par]], w=[pY0])
                    MM(pY1[:, hsl], wg2b[:, k * 128:(k + 1) * 128],
                       xbf[par][:, hsl], st0, sp0, r=[wg2b, xbf[par]], w=[pY1])
                    MM(pDP[0:1, hsl], onescb[:, :], xsq[par][:, hsl],
                       st0, sp0, r=[onescb, xsq[par]], w=[pDP])
            for hh in range(2):
                dsl = slice(tcg * 1024 + hh * 512, tcg * 1024 + hh * 512 + 512)
                hsl = slice(hh * 512, hh * 512 + 512)
                ACT(AB[0:64, dsl], pY0[0:64, hsl], AF.Copy, r=[pY0], w=[AB])
                ACT(AB[64:128, dsl], pY0[64:128, hsl], AF.Copy, r=[pY0], w=[AB])
                ACT(CD[0:64, dsl], pY1[0:64, hsl], AF.Copy, r=[pY1], w=[CD])
                ACT(CD[64:128, dsl], pY1[64:128, hsl], AF.Copy, r=[pY1], w=[CD])
                ACT(pack2[0:1, dsl], pDP[0:1, hsl], AF.Copy, r=[pDP], w=[pack2])

        DMA("sync", arr_C, a_hi[16:32, :], r=[AB], w=[S1])
        DMA("sync", arr_x8, a_hi[32:40, :], r=[AB], w=[S2])
        DMA("sync", pack2[1:2, :], a_hi[40:41, :], r=[AB], w=[pack2])

        for col, wsrc in ((0, wg1b), (1, wg2b)):
            for k in range(NKC):
                MM(pSM[:, col:col + 1], wsrc[:, k * 128:(k + 1) * 128],
                   onescb[:, :], k == 0, k == NKC - 1, r=[wsrc, onescb], w=[pSM])
        for col, wsrc in ((2, mdTb), (3, mdsqb)):
            for k in range(NKC):
                MM(pSM[0:8, col:col + 1], wsrc[:, k * 8:(k + 1) * 8],
                   onescb[:, :], k == 0, k == NKC - 1, r=[wsrc, onescb], w=[pSM])
        ACT(ws_g1[:, 0:1], pSM[:, 0:1], AF.Copy, r=[pSM], w=[ws_g1])
        ACT(ws_g2[:, 0:1], pSM[:, 1:2], AF.Copy, r=[pSM], w=[ws_g2])
        ACT(mdsqs[:, 0:1], pSM[0:8, 3:4], AF.Copy, r=[pSM], w=[mdsqs])
        VTS(nws_g1[:], ws_g1[:], -1.0, ALU.mult, r=[ws_g1], w=[nws_g1])
        VTS(nws_g2[:], ws_g2[:], -1.0, ALU.mult, r=[ws_g2], w=[nws_g2])
        DMA("sync", nws2[32:48, :], nws_g1[80:96, :], r=[nws_g1], w=[nws2])
        DMA("sync", nws3[32:40, :], nws_g1[96:104, :], r=[nws_g1], w=[nws3])

        for k in range(NKC):
            MM(pTR[0:64, 0:8], wg2b[:, k * 128:k * 128 + 64],
               mdTb[:, k * 8:(k + 1) * 8], k == 0, k == NKC - 1,
               r=[wg2b, mdTb], w=[pTR])
        for k in range(NKC):
            MM(pTR[0:64, 8:16], wg2b[:, k * 128 + 64:(k + 1) * 128],
               dbTb[:, k * 8:(k + 1) * 8], k == 0, k == NKC - 1,
               r=[wg2b, dbTb], w=[pTR])
        ACT(cross[:, :], pTR[0:64, 0:16], AF.Copy, r=[pTR], w=[cross])
        for n in range(8):
            DMA("sync", c1c2[n * 8:(n + 1) * 8, 0:1],
                cross[n * 8:(n + 1) * 8, n:n + 1], r=[cross], w=[c1c2])
            DMA("sync", c1c2[n * 8:(n + 1) * 8, 1:2],
                cross[n * 8:(n + 1) * 8, 8 + n:9 + n], r=[cross], w=[c1c2])

        for q in range(NQ):
            qoff = q * TQ
            tsl = slice(qoff, qoff + TQ)
            bcast2([rp[0], rp[1]], qoff)
            ln_chain(rp[0], rp[1], rp[2], mu_out=rp[3])
            for (arr_ap, stripe, base, rows, nws_ap, nws_h) in (
                    (a_lo, AB, 0, 64, nws_g1[0:64, 0:1], nws_g1),
                    (a_hi[0:16, :], AB, 64, 16, nws_g1[64:80, 0:1], nws_g1),
                    (arr_C, S1, 32, 16, nws2[32:48, 0:1], nws2),
                    (arr_x8, S2, 32, 8, nws3[32:40, 0:1], nws3),
                    (a_ws, CD, 0, 64, nws_g2[0:64, 0:1], nws_g2),
                    (a_u6, CD, 64, 64, nws_g2[64:128, 0:1], nws_g2)):
                VSTT(arr_ap[:, tsl], rp[3][base:base + rows, :], nws_ap,
                     arr_ap[:, tsl], ALU.mult, ALU.add,
                     r=[rp[3], stripe, nws_h], w=[stripe])
                VTT(arr_ap[:, tsl], arr_ap[:, tsl], rp[0][base:base + rows, :],
                    ALU.mult, r=[stripe, rp[0]], w=[stripe])

        VTS(bnegb, a_hi[0:16, :], 1.0, ALU.mult, r=[AB], w=[Sbf1])
        ACT(crowb, arr_C, AF.Copy, r=[S1], w=[Sbf3])
        ACT(xdbr, a_lo, AF.Copy, r=[AB], w=[Sbf2])

        VTS(a_ws, a_ws, c1c2[:, 0:1], ALU.subtract, r=[CD, c1c2], w=[CD])
        ACT(a_ws, a_ws, AF.Square, r=[CD], w=[CD])
        for c in range(8):
            csl = slice(c * 512, (c + 1) * 512)
            MM(pSM[0:8, 0:512], selsf[:, :], a_ws[:, csl], True, True,
               r=[selsf, CD], w=[pSM])
            ACT(g_sc[0:8, csl], pSM[0:8, 0:512], AF.Copy, r=[pSM], w=[AB])
        DMA("sync", arr_x8c, arr_x8, r=[S2], w=[S2])
        for q in range(NQ):
            qoff = q * TQ
            bcast2([rp[0], rp[1]], qoff)
            ln_chain(rp[0], rp[1], rp[2], mu_out=rp[3])
            bcast2([rp[2]], qoff)
            VTT(rp[3][:], rp[3][:], rp[3][:], ALU.mult, r=[rp[3]], w=[rp[3]])
            VSTT(rp[2][:], rp[3][:], -float(D), rp[2][:], ALU.mult, ALU.add,
                 r=[rp[3], rp[2]], w=[rp[2]])
            VTT(rp[2][:], rp[2][:], rp[0][:], ALU.mult, r=[rp[2], rp[0]], w=[rp[2]])
            VTT(rp[2][:], rp[2][:], rp[0][:], ALU.mult, r=[rp[2], rp[0]], w=[rp[2]])
            for hh in range(2):
                csl = slice(qoff + hh * 512, qoff + hh * 512 + 512)
                rsl = slice(hh * 512, hh * 512 + 512)
                VSTT(g_nsq[:, csl], arr_x8c[:, csl], -2.0, rp[2][0:8, rsl],
                     ALU.mult, ALU.add, r=[S2, rp[2]], w=[S1])
        VTS(g_nsq, g_nsq, mdsqs[:, 0:1], ALU.add, r=[S1, mdsqs], w=[S1])
        VREC(g_nsq, g_nsq, r=[S1], w=[S1])
        VTT(g_sc, g_sc, g_nsq, ALU.mult, r=[AB, S1], w=[AB])
        ACT(g_gate, g_sc, AF.Sigmoid, r=[AB, slpf, nslpc], w=[S2],
            scale=slpf[:, 0:1], bias=nslpc[:, 0:1])

        for half in range(2):
            for j in range(16):
                c = half * 16 + j
                TRANS(pTR[0:128, j * 8:j * 8 + 8], g_gate[:, c * 128:(c + 1) * 128],
                      r=[S2], w=[pTR])
            ACT(gT[:, half * 128:half * 128 + 128], pTR[:, 0:128], AF.Copy,
                r=[pTR], w=[gT])
        for c in range(NTT):
            tg = gT[:, c * 8:(c + 1) * 8]
            VRED(ssc[:, c:c + 1], tg, ALU.max, r=[gT], w=[ssc])
            VTS(ohT[:, c * 8:(c + 1) * 8], tg, ssc[:, c:c + 1], ALU.is_equal,
                r=[gT, ssc], w=[ohT])
            VSTT(g_v[:, c * 8:(c + 1) * 8], ohT[:, c * 8:(c + 1) * 8], -16.0,
                 iotaf[:, 0:8], ALU.mult, ALU.add, r=[ohT, iotaf], w=[g_v])
            VRED(idxc[:, c:c + 1], g_v[:, c * 8:(c + 1) * 8], ALU.min,
                 r=[g_v], w=[idxc])
            VTS(ohT[:, c * 8:(c + 1) * 8], g_v[:, c * 8:(c + 1) * 8],
                idxc[:, c:c + 1], ALU.is_equal, r=[g_v, idxc], w=[ohT])
        for batch in range(8):
            for j in range(4):
                c = batch * 4 + j
                TRANS(pTR[0:8, j * 128:(j + 1) * 128], ohT[:, c * 8:(c + 1) * 8],
                      r=[ohT], w=[pTR])
            ACT(oh1[0:8, batch * 512:(batch + 1) * 512], pTR[0:8, 0:512], AF.Copy,
                r=[pTR], w=[Sbf3])

        DMA("sync", AB[0:64, :], a_u6, r=[CD, AB], w=[AB])
        VTS(AB[0:64, :], AB[0:64, :], c1c2[:, 1:2], ALU.subtract,
            r=[AB, c1c2], w=[AB])
        for c in range(8):
            csl = slice(c * 512, (c + 1) * 512)
            MM(pSM[0:64, 0:512], rep8b[:, :], oh1[0:8, csl], True, True,
               r=[rep8b, Sbf3], w=[pSM])
            ACT(rp[3][0:64, 0:512], pSM[0:64, 0:512], AF.Copy, r=[pSM], w=[rp[3]])
            VTT(mtil[0:64, csl], AB[0:64, csl], rp[3][0:64, 0:512], ALU.mult,
                r=[AB, rp[3]], w=[Sbf1])

        def tap(name, src_ap, rhs):
            if name in tap_aps:
                DMA("gpsimd", tap_aps[name], src_ap, r=rhs)

        tap("t_gate", g_gate, [S2])
        tap("t_nsq", g_nsq, [S1])
        tap("t_ssc", ssc[:], [ssc])
        tap("t_idx", idxc[:], [idxc])
        tap("t_ax8", arr_x8, [S2])
        tap("t_brow", a_hi[0:16, :], [AB])
        tap("t_arrC", arr_C, [S1])
        tap("t_pack2", pack2[:], [pack2])
        tap("t_mtf", AB[0:64, :], [AB])
        tap("t_dr", a_lo, [AB])

        sch.barrier()
        esB.close()

        esC = ExitStack()
        SC = lambda n, sh, dtp=F32: _alloc(esC, "sbuf", n, sh, dtp)
        brep = [SC(f"brep{n}", [128, TQ], BF16) for n in range(NST)]
        crep = [SC(f"crep{n}", [128, TQ], BF16) for n in range(NST)]
        dneg = [SC(f"dneg{i}", [128, TQ]) for i in range(2)]
        xnl = [SC(f"xnl{i}", [128, TQ]) for i in range(2)]
        dxb = [SC(f"dxb{i}", [128, TQ], BF16) for i in range(2)]
        dAt = [SC(f"dAt{i}", [128, TQ]) for i in range(3)]
        ubt = [SC(f"ubt{i}", [128, TQ], BF16) for i in range(2)]
        hbt = [SC(f"hbt{i}", [128, TQ], BF16) for i in range(2)]
        xtt = [SC(f"xtt{i}", [128, TQ]) for i in range(2)]
        murq = SC("murq", [128, TQ])
        rrq = SC("rrq", [128, TQ])
        scrB = SC("scrB", [16, TQ], BF16)
        scrC = SC("scrC", [16, TQ], BF16)
        sl1 = SC("sl1", [128, TQ])
        rm2 = sl1
        sTq = SC("sTq", [128, NDC * TQ], BF16)
        xnt = [xtt[1], xtt[1]]
        ep1_ = SC("ep1_", [128, DL])
        rel_ = ep1_
        ep1 = [ep1_, ep1_]
        ep2_ = SC("ep2_", [128, DL])
        ep2 = [ep2_, ep2_]

        def bc_rep(dst, scr, scrh, n):
            for hh in range(2):
                pb = pSM if hh == 0 else pTR
                MM(pb[:, 0:512], sel16b[:, n * 128:(n + 1) * 128],
                   scr[0:16, hh * 512:hh * 512 + 512], True, True,
                   r=[sel16b, scrh], w=[pb])
                ACT(dst[:, hh * 512:hh * 512 + 512], pb[:, 0:512], AF.Copy,
                    r=[pb], w=[dst])

        def emit_epi_a(eq, tt):
            tpar = tt % 2
            gt = eq * 8 + tt
            rowsl = slice(eq * TQ + tt * 128, eq * TQ + tt * 128 + 128)
            DMA("sync", xnt[tpar][:], xN[rowsl, :], w=[xnt[tpar]])
            for hh in range(2):
                hsl = slice(hh * 512, hh * 512 + 512)
                MM(pDP[:, hsl], mtil[0:64, gt * 128:(gt + 1) * 128],
                   dgtb[:, hsl], True, False, r=[Sbf1, dgtb], w=[pDP])
                MM(pDP[:, hsl], oh1[0:8, gt * 128:(gt + 1) * 128],
                   bwlb[:, hsl], False, True, r=[Sbf3, bwlb], w=[pDP])
            VTS(ep1[tpar][:], xnt[tpar][:], statc[:, 2 * gt:2 * gt + 1],
                ALU.mult, r=[xnt[tpar], statc], w=[ep1[tpar]],
                s2=statc[:, 2 * gt + 1:2 * gt + 2], op1=ALU.subtract)
            ACT(ep2[tpar][:, 0:512], pDP[:, 0:512], AF.Copy,
                r=[pDP], w=[ep2[tpar]])
            ACT(ep2[tpar][:, 512:1024], pDP[:, 512:1024], AF.Copy,
                r=[pDP], w=[ep2[tpar]])
            VTT(ep2[tpar][:], ep2[tpar][:], ep1[tpar][:], ALU.subtract,
                r=[ep2[tpar], ep1[tpar]], w=[ep2[tpar]])
            VSTT(ep2[tpar][:], ep2[tpar][:], ssc[:, gt:gt + 1],
                 ep1[tpar][:], ALU.mult, ALU.add,
                 r=[ep2[tpar], ssc, ep1[tpar]], w=[ep2[tpar]])
            ACT(ep2[tpar][:], ep2[tpar][:], AF.Silu,
                r=[ep2[tpar]], w=[ep2[tpar]])

        def emit_epi_b(eq, tt):
            tpar = tt % 2
            rowsl = slice(eq * TQ + tt * 128, eq * TQ + tt * 128 + 128)
            sview = sTq[:].rearrange("p (d t) -> p d t", d=NDC)[
                :, :, tt * 128:(tt + 1) * 128]
            gview = ep2[tpar][:].rearrange("p (d t) -> p d t", d=NDC)
            oview = ep1[tpar][:].rearrange("p (d t) -> p d t", d=NDC)
            sch.op("vector", lambda e, ov=oview, sv=sview, gv=gview:
                   e.tensor_tensor(out=ov, in0=sv, in1=gv, op=ALU.mult),
                   bl([sTq, ep2[tpar]]), bl([ep1[tpar]]))
            VTT(ep1[tpar][:], ep1[tpar][:], xnt[tpar][:], ALU.add,
                r=[ep1[tpar], xnt[tpar]], w=[ep1[tpar]])
            DMA("sync", out[rowsl, :], ep1[tpar][:], r=[ep1[tpar]])

        def emit_epi(eq, tt):
            emit_epi_a(eq, tt)
            emit_epi_b(eq, tt)

        def qpro_0(tq):
            bcast2([rrq, murq], tq * TQ)
            ln_chain(rrq, murq, rel_)

        def qpro_1(tq):
            for j in range(8):
                TRANS(pTR[0:128, j * 2:j * 2 + 1],
                      rrq[0:1, j * 128:(j + 1) * 128], r=[rrq], w=[pTR])
                TRANS(pTR[0:128, j * 2 + 1:j * 2 + 2],
                      murq[0:1, j * 128:(j + 1) * 128], r=[murq], w=[pTR])
            ACT(statc[:, tq * 16:tq * 16 + 16], pTR[:, 0:16], AF.Copy,
                r=[pTR], w=[statc])
            tsl = slice(tq * TQ, tq * TQ + TQ)
            DMA("sync", scrB[:], bnegb[0:16, tsl], r=[Sbf1], w=[scrB])
            DMA("sync", scrC[:], crowb[0:16, tsl], r=[Sbf3], w=[scrC])

        pending_epi = None
        pending_tail = None
        for q in range(NQ):
            qoff = q * TQ
            qsl = slice(qoff, qoff + TQ)
            if q == 0:
                qpro_0(0)
                qpro_1(0)
                for n in range(2):
                    bc_rep(brep[n][:], scrB[:], scrB, n)
                    bc_rep(crep[n][:], scrC[:], scrC, n)

            def dpro_a0(d, toff):
                MM(pDP[:, 0:512], dtwb[:, d * 128:(d + 1) * 128],
                   xdbr[0:64, toff:toff + 512], True, True, r=[dtwb, Sbf2], w=[pDP])
                MM(pDP[:, 512:1024], dtwb[:, d * 128:(d + 1) * 128],
                   xdbr[0:64, toff + 512:toff + 1024], True, True,
                   r=[dtwb, Sbf2], w=[pDP])

            def dpro_a1(d):
                dp = d % 2
                ACT(dneg[dp][:], pDP[:, :], AF.Abs, r=[pDP, posb],
                    w=[dneg[dp]], bias=posb[:, d:d + 1])

            def dpro_a2(d):
                dp = d % 2
                ACT(dneg[dp][:], dneg[dp][:], AF.Exp, scale=-1.0,
                    r=[dneg[dp]], w=[dneg[dp]])

            def dpro_a3(d):
                ACT(rel_[:], pDP[:, :], AF.Relu, r=[pDP, posb],
                    w=[rel_], bias=posb[:, d:d + 1])

            def dpro_a4(d):
                dp = d % 2
                ACT(dneg[dp][:], dneg[dp][:], AF.Ln, bias=1.0,
                    r=[dneg[dp]], w=[dneg[dp]])
                sch.op("gpsimd", lambda e, o=dneg[dp][:], a=rel_[:]:
                       e.tensor_tensor(out=o, in0=o, in1=a, op=ALU.add),
                       bl([dneg[dp], rel_]), bl([dneg[dp]]))

            def emit_dpro_a(d, toff):
                dpro_a0(d, toff)
                dpro_a1(d)
                dpro_a2(d)
                dpro_a3(d)
                dpro_a4(d)

            def dpro_b_dma(d, toff):
                dp = d % 2
                DMA("scalar", xtt[dp][:], xT[d * 128:(d + 1) * 128,
                                             toff:toff + TQ],
                    w=[xtt[dp]])

            def emit_dpro_b(d, toff, with_dma=True):
                dp = d % 2
                if with_dma:
                    dpro_b_dma(d, toff)
                VTT(xnl[dp][:], xtt[dp][:], rrq[:], ALU.mult,
                    r=[xtt[dp], rrq], w=[xnl[dp]])
                VTT(xnl[dp][:], xnl[dp][:], murq[:], ALU.subtract,
                    r=[xnl[dp], murq], w=[xnl[dp]])
                VTT(dxb[dp][:], dneg[dp][:], xnl[dp][:], ALU.mult,
                    r=[dneg[dp], xnl[dp]], w=[dxb[dp]])

            def emit_dtail_a(d_, pyb_, dp_):
                ACT(sl1[:, 0:512], pyb_[:, 0:512], AF.Copy, r=[pyb_], w=[sl1])
                ACT(sl1[:, 512:1024], pyb_[:, 512:1024], AF.Copy,
                    r=[pyb_], w=[sl1])
                VSTT(sl1[:], xnl[dp_][:], dvecf[:, d_:d_ + 1], sl1[:],
                     ALU.mult, ALU.add, r=[xnl[dp_], dvecf, sl1], w=[sl1])

            def emit_dtail_b(d_, pyb_, dp_, g4):
                for j in range(4):
                    tt = g4 * 4 + j
                    TRANS(pTR[:, j * 128:(j + 1) * 128],
                          sl1[:, tt * 128:(tt + 1) * 128],
                          r=[sl1], w=[pTR])
                ACT(sTq[:, d_ * TQ + g4 * 512:d_ * TQ + g4 * 512 + 512],
                    pTR[:, 0:512], AF.Copy, r=[pTR], w=[sTq])

            def emit_dtail_trans(d_, pyb_, dp_):
                for j in range(4):
                    TRANS(pTR[:, j * 128:(j + 1) * 128],
                          sl1[:, j * 128:(j + 1) * 128], r=[sl1], w=[pTR])
                for j in range(4):
                    TRANS(pSM[:, j * 128:(j + 1) * 128],
                          sl1[:, (4 + j) * 128:(5 + j) * 128],
                          r=[sl1], w=[pSM])

            def emit_dtail_copy(d_, pyb_, dp_, g4):
                pb = pTR if g4 == 0 else pSM
                ACT(sTq[:, d_ * TQ + g4 * 512:d_ * TQ + g4 * 512 + 512],
                    pb[:, 0:512], AF.Copy, r=[pb], w=[sTq])

            if q == 0:
                emit_dpro_a(0, qoff)
                emit_dpro_b(0, qoff)
            for d in range(NDC):
                dp = d % 2
                pyb = pY0 if d % 2 == 0 else pY1
                for n in range(NST):
                    npar = n % 2
                    col = d * NST + n
                    ai = col % 3
                    if n == 14 and d >= 1 and pending_tail is not None:
                        emit_dtail_trans(*pending_tail)
                    VTT(ubt[npar][:], dxb[dp][:], brep[n][:], ALU.mult,
                        r=[dxb[dp], brep[n]], w=[ubt[npar]])
                    if not (n < 3 and (d > 0 or q > 0)):
                        ACT(dAt[ai][:], dneg[dp][:], AF.Exp,
                            r=[dneg[dp], anegs], w=[dAt[ai]],
                            scale=anegs[:, col:col + 1])
                    sch.op("vector", lambda e, o=hbt[npar][:], a=dAt[ai][:],
                           b=ubt[npar][:], ini=st_[:, col:col + 1]:
                           e.tensor_tensor_scan(out=o, data0=a, data1=b,
                                                initial=ini, op0=ALU.mult,
                                                op1=ALU.add),
                           bl([dAt[ai], ubt[npar]]) + [stcol[col]],
                           bl([hbt[npar]]))
                    sch.op("gpsimd", lambda e, o=st_[:, col:col + 1],
                           i=hbt[npar][:, TQ - 1:TQ]: e.tensor_copy(out=o, in_=i),
                           bl([hbt[npar]]), [stcol[col]])
                    VTT(ubt[npar][:], hbt[npar][:], crep[n][:], ALU.mult,
                        r=[hbt[npar], crep[n]], w=[ubt[npar]])
                    MM(pyb[:, 0:512], idenb[:, :], ubt[npar][:, 0:512],
                       n == 0, n == NST - 1, r=[idenb, ubt[npar]], w=[pyb])
                    MM(pyb[:, 512:1024], idenb[:, :], ubt[npar][:, 512:1024],
                       n == 0, n == NST - 1, r=[idenb, ubt[npar]], w=[pyb])
                    if n == 2 and pending_tail is not None:
                        emit_dtail_a(*pending_tail)
                    if d == 0:
                        if n == 5 and pending_tail is not None:
                            emit_dtail_b(*pending_tail, 0)
                        if n == 6 and pending_tail is not None:
                            emit_dtail_b(*pending_tail, 1)
                            pending_tail = None
                    else:
                        if n == 14 and pending_tail is not None:
                            emit_dtail_copy(*pending_tail, 0)
                            emit_dtail_copy(*pending_tail, 1)
                            pending_tail = None
                    if q == 0 and d == 0 and n + 2 < NST:
                        bc_rep(brep[n + 2][:], scrB[:], scrB, n + 2)
                        bc_rep(crep[n + 2][:], scrC[:], scrC, n + 2)
                    if d == NDC - 2 and q + 1 < NQ:
                        if n == 14:
                            qpro_0(q + 1)
                        elif n == 15:
                            qpro_1(q + 1)
                    if d == NDC - 1 and q + 1 < NQ:
                        if n >= 3:
                            bc_rep(brep[n - 3][:], scrB[:], scrB, n - 3)
                            bc_rep(crep[n - 3][:], scrC[:], scrC, n - 3)
                    if (pending_epi is not None and d <= 1
                            and n in (7, 9, 11, 13)):
                        emit_epi_b(pending_epi, d * 4 + (n - 7) // 2)
                    if d + 1 < NDC:
                        td, toff = d + 1, qoff
                    elif q + 1 < NQ:
                        td, toff = 0, qoff + TQ
                    else:
                        td = None
                    if td is not None:
                        if n == 2:
                            dpro_a0(td, toff)
                            if d >= 1:
                                dpro_b_dma(td, toff)
                        elif n == 3:
                            dpro_a1(td)
                        elif n == 4:
                            dpro_a2(td)
                        elif n == 5:
                            dpro_a3(td)
                        elif n == 6:
                            dpro_a4(td)
                        elif n == 9:
                            emit_dpro_b(td, toff, with_dma=(d == 0))
                    if (pending_epi is not None and d <= 1
                            and n in (6, 8, 10, 12)):
                        emit_epi_a(pending_epi, d * 4 + (n - 6) // 2)

                if d + 1 < NDC:
                    ndp, nbase = (d + 1) % 2, (d + 1) * NST
                elif q + 1 < NQ:
                    ndp, nbase = 0, 0
                else:
                    ndp = None
                if ndp is not None:
                    for nn in range(3):
                        ncol = nbase + nn
                        ACT(dAt[ncol % 3][:], dneg[ndp][:], AF.Exp,
                            r=[dneg[ndp], anegs], w=[dAt[ncol % 3]],
                            scale=anegs[:, ncol:ncol + 1])
                if d == NDC - 1 and q + 1 < NQ:
                    for nn in (13, 14, 15):
                        bc_rep(brep[nn][:], scrB[:], scrB, nn)
                        bc_rep(crep[nn][:], scrC[:], scrC, nn)
                pending_tail = (d, pyb, dp)

            if ("t_sT%d" % q) in tap_aps:
                tap("t_sT%d" % q, sTq[:], [sTq])
            pending_epi = q

        if pending_tail is not None:
            emit_dtail_a(*pending_tail)
            emit_dtail_b(*pending_tail, 0)
            emit_dtail_b(*pending_tail, 1)
            pending_tail = None
        xnt[0] = xtt[0]
        for tt in range(8):
            emit_epi(NQ - 1, tt)

        allw = [(s, sch.count[s]) for s in Sched.SEMS if sch.count[s]]
        sch.op("gpsimd", lambda e: e.nop(), [], [], extra=allw)

        esC.close()
        sch.build(block)

    es.close()
    return nc, tap_aps


def _chunked(a):
    k = a.shape[0] // 128
    return np.ascontiguousarray(
        a.reshape(k, 128, a.shape[1]).transpose(1, 0, 2).reshape(128, -1))


def _stage_inputs(inputs):
    f = np.float32
    x = np.asarray(inputs["L0_attn_x"], dtype=f)
    xpw = np.asarray(inputs["L0_attn_x_proj_w"], dtype=f)
    dtw_w = np.asarray(inputs["L0_attn_dt_proj_w"], dtype=f)
    dtb_v = np.asarray(inputs["L0_attn_dt_proj_b"], dtype=f)
    alog_w = np.asarray(inputs["L0_attn_A_log"], dtype=f)
    dvec_v = np.asarray(inputs["L0_attn_D"], dtype=f)
    selw = np.asarray(inputs["L0_ffn_select_weight"], dtype=f)
    md = np.asarray(inputs["L0_ffn_select_mean_diff"], dtype=f)
    ictr = np.asarray(inputs["L0_ffn_imp_center"], dtype=f)
    islp = np.asarray(inputs["L0_ffn_imp_slope"], dtype=f)
    lup = np.asarray(inputs["L0_ffn_lora_update"], dtype=f)
    ldg = np.asarray(inputs["L0_ffn_lora_degen"], dtype=f)
    bw = np.asarray(inputs["L0_ffn_bias_w"], dtype=f)
    dbw = np.asarray(inputs["L0_ffn_debias_w"], dtype=f)

    wsel_flat = selw.transpose(1, 0, 2).reshape(D, 64)
    u_flat = lup.transpose(1, 0, 2).reshape(D, 64)
    dg_flat = ldg.transpose(0, 2, 1).reshape(64, D)
    g1 = np.concatenate([xpw.T, md.T, np.ones((D, 1), f),
                         np.zeros((D, 23), f)], axis=1)
    g2 = np.concatenate([wsel_flat, u_flat], axis=1)

    iden = np.eye(128, dtype=f)
    iota16 = np.tile(np.arange(8, dtype=f) + 16.0, (128, 1))
    onesc = np.ones((128, 1), f)
    sel2 = np.zeros((2, 256), f)
    sel2[0, 0:128] = 1.0
    sel2[1, 128:256] = 1.0
    rep8 = np.zeros((8, 64), f)
    for n in range(8):
        rep8[n, n * 8:(n + 1) * 8] = 1.0
    sels = np.zeros((64, 8), f)
    for n in range(8):
        sels[n * 8:(n + 1) * 8, n] = 1.0
    sel16 = np.zeros((16, 16 * 128), f)
    for n in range(16):
        sel16[n, n * 128:(n + 1) * 128] = 1.0

    in_maps = []
    for c in range(NCORES):
        b, half = c // 2, c % 2
        lo = half * DL
        perm = np.concatenate([np.arange(lo, lo + DL),
                               np.arange((1 - half) * DL, (1 - half) * DL + DL)])
        xb = x[b]
        in_maps.append({
            "xT": np.ascontiguousarray(xb.T[perm]),
            "xN": np.ascontiguousarray(xb[:, lo:lo + DL]),
            "wg1": _chunked(g1[perm]),
            "wg2": _chunked(g2[perm]),
            "dtw": np.ascontiguousarray(dtw_w.T[:, lo:lo + DL]),
            "dgt": np.ascontiguousarray(dg_flat[:, lo:lo + DL]),
            "bwl": np.ascontiguousarray(bw[:, lo:lo + DL]),
            "mdTd": _chunked(md.T[perm]),
            "dbTd": _chunked(dbw.T[perm]),
            "alog": _chunked(alog_w[lo:lo + DL]),
            "dtb": _chunked(dtb_v[lo:lo + DL, None]),
            "dvec": _chunked(dvec_v[lo:lo + DL, None]),
            "slp": np.ascontiguousarray(islp[:, None]),
            "ctr": np.ascontiguousarray(ictr[:, None]),
            "iden": iden, "iota16": iota16, "onesc": onesc, "sel2": sel2,
            "rep8": rep8, "sels": sels, "sel16": sel16,
        })
    return in_maps


def kernel(**inputs):
    in_maps = _stage_inputs(inputs)
    nc, _ = build_nc()
    res = run_bass_kernel_spmd(nc, in_maps, core_ids=list(range(NCORES)))
    outp = np.empty((4, T, D), dtype=np.float32)
    for c in range(NCORES):
        b, half = c // 2, c % 2
        lo = half * DL
        outp[b, :, lo:lo + DL] = np.asarray(res.results[c]["out"])
    return outp


if __name__ == "__main__":
    nc, _ = build_nc()
    print("built ok")
